# version 15
# baseline (speedup 1.0000x reference)
import numpy as np
import ml_dtypes
import concourse.bass as bass
import concourse.mybir as mybir
from concourse.bass_utils import run_bass_kernel_spmd

F32 = mybir.dt.float32
BF16 = mybir.dt.bfloat16
AF = mybir.ActivationFunctionType
ALU = mybir.AluOpType
AX = mybir.AxisListType
BF = ml_dtypes.bfloat16

T = 2048
D = 4096
NC = D // 128
NT = T // 128
NS = 16
DTR = 128
E = 8
RK = 16
TH = T // 2
EPS = 1e-5
MAXTHR = 0.125
SCALING = 2.0

ENGS = ("sync", "tensor", "vector", "scalar", "gpsimd")
ESEM = {"sync": "sy", "tensor": "pe", "vector": "dv", "scalar": "ac", "gpsimd": "po"}
BSEM = {"sync": "bsy", "tensor": "bpe", "vector": "bdv", "scalar": "bac", "gpsimd": "bpo"}
DSEMS = ("lc", "lx", "lw", "lw1", "lr", "lxm", "lrs", "ldt0", "ldt1", "lg",
         "s0", "sw", "sf", "sf1", "sb_", "se", "sb2", "so", "stp",
         "bl0", "bl1", "bl2", "bl3", "bl4", "bl5", "bl6", "bl7", "lg1", "lwg")


def build_program(debug_taps=(), max_phase=99, scan_taps=False):
    nc = bass.Bass("TRN2", target_bir_lowering=False, debug=False)

    di = lambda n, s: nc.dram_tensor(n, s, F32, kind="ExternalInput").ap()
    dib = lambda n, s: nc.dram_tensor(n, s, BF16, kind="ExternalInput").ap()
    x_in = di("x", [T, D])
    A_log = di("A_log", [D, NS])
    Dskip = di("Dskip", [D])
    x_proj_w = di("x_proj_w", [DTR + 2 * NS, D])
    dt_proj_w = di("dt_proj_w", [D, DTR])
    dt_proj_b = di("dt_proj_b", [D])
    W_base = di("W_base", [D, D])
    b_base = di("b_base", [D])
    W_gate = di("W_gate", [E, D])
    W_thr = di("W_thr", [D])
    b_thr = di("b_thr", [1])
    A_w = di("A_w", [E * RK, D])
    B_w = di("B_w", [E * D, RK])
    ident_f = di("ident_f", [128, 128])
    ident_bd = di("ident_b", [128, 128])
    ones_bd = di("ones_b", [128, 128])
    indic_bd = di("indic_b", [16, 16 * 128])
    expand_bd = di("expand_b", [16, 128])
    out_d = nc.dram_tensor("out", [T, D], F32, kind="ExternalOutput").ap()

    ds = lambda n, s, dt=F32: nc.dram_tensor(n, s, dt, kind="Internal").ap()
    sc_r0 = ds("sc_r0", [D, T])
    sc_r1 = ds("sc_r1", [D, T])
    sc_r1b = ds("sc_r1b", [D, T], BF16)
    sc_r2 = ds("sc_r2", [D, T])
    sc_r2b = ds("sc_r2b", [D, T], BF16)
    sc_r3 = ds("sc_r3", [D, T])
    sc_xmb = ds("sc_xmb", [D, T], BF16)
    sc_wt = ds("sc_wt", [NC, 128, D], BF16)
    sc_dtp = ds("sc_dtp", [NC, 128, 128], BF16)
    sc_bft = ds("sc_bft", [128, D], BF16)
    sc_aft = ds("sc_aft", [128, D], BF16)
    sc_wgt = ds("sc_wgt", [128, 288], BF16)

    sb = lambda n, s, dt=F32: nc.alloc_sbuf_tensor(n, s, dt).ap()
    identf = sb("identf", [128, 128])
    identb = sb("identb", [128, 128], BF16)
    ones128 = sb("ones128", [128, 128], BF16)
    indic = sb("indic", [16, 16 * 128], BF16)
    expand = sb("expand", [16, 128], BF16)
    A_sc = sb("A_sc", [128, NC * NS])
    D_col = sb("D_col", [128, NC])
    dtb_col = sb("dtb_col", [128, NC])
    bb_col = sb("bb_col", [128, NC])
    thrb = sb("thrb", [128, 1])
    dtin = sb("dtin", [128, T], BF16)
    dtpbuf = sb("dtpbuf", [128, 256], BF16)
    colt = sb("colt", [128, 8])
    ceps = sb("ceps", [128, 1])
    tmpc = sb("tmpc", [128, 1024])

    big = sb("big", [128, 65536], BF16)
    xln = big[:, 0:NC * TH]
    wld_f = big[:, 32768:40960].bitcast(F32)
    wtt = [big[:, 40960:45056], big[:, 45056:49152]]
    ln_stg = big[:, 40960:45056]
    g_hw = big[:, 49152:50176]
    g_h = big[:, 50176:51200]
    AfT = big[:, 51200:55296]
    xpT = big[:, 55296:60416]
    WgTb = big[:, 60416:60704]
    wcast = big[:, 60704:64800]
    dgB = lambda n, tau: big[:, n * T + tau * 128:n * T + (tau + 1) * 128]
    Cb_n = lambda n: big[:, 32768 + n * T:32768 + (n + 1) * T]

    poolA = sb("poolA", [128, 6144])
    pA = lambda i: poolA[:, i * 2048:(i + 1) * 2048]
    poolC = sb("poolC", [128, 2048])
    pC = lambda i: poolC[:, i * 2048:(i + 1) * 2048]
    poolD = sb("poolD", [128, 4096], BF16)
    pD = lambda i: poolD[:, i * 2048:(i + 1) * 2048]
    poolE = sb("poolE", [128, 8192], BF16)
    BfT = poolE[:, 0:4096]
    hbuf = lambda p: poolE[:, p * 2048:(p + 1) * 2048]
    xbuf = lambda p: poolE[:, (2 + p) * 2048:(3 + p) * 2048]
    Bsb = poolE[0:16, 0:2048]
    Csb = poolE[0:16, 2048:4096]
    poolF = sb("poolF", [128, 2048], BF16)
    r2stg = tmpc.bitcast(BF16)
    cbuf = lambda p: poolF[:, 0:2048]
    stage = poolA[0:32, 0:4096]
    stage2 = poolF[0:16, 0:2048]
    stageB = poolE[0:16, 4096:6144]
    wldA = poolA[:, 0:4096]
    wcast2 = poolE[:, 4096:8192]

    psALL = nc.alloc_psum_tensor("psALL", [128, 4096], F32).ap()
    psA = psALL[:, 0:2048]
    psB = psALL[:, 2048:3072]
    psC = psALL[:, 3072:3584]
    psD = psALL[:, 3584:4096]
    psDb = psD.bitcast(BF16)
    psCb = psC.bitcast(BF16)
    dbxps = lambda p: psALL[:, p * 2048:(p + 1) * 2048]

    S = {k: nc.alloc_semaphore(k) for k in
         ("pe", "dv", "ac", "po", "sy", "bpe", "bdv", "bac", "bpo", "bsy") + DSEMS}
    C = {k: 0 for k in S}
    prog = {e: [] for e in ENGS}

    def on(eng, f):
        prog[eng].append(f)

    def wait(eng, sem, val):
        if val > 0:
            on(eng, lambda E_, s=S[sem], v=val: E_.wait_ge(s, v))

    def do(eng, f, inc=None, n=1):
        es = ESEM[eng]
        prev = C[es]
        selfwait = (eng != "tensor")
        if inc is None:
            if selfwait and prev > 0:
                on(eng, lambda E_, f=f, s=S[es], p=prev: f(E_)._wait_ge(s, p))
            else:
                on(eng, f)
            return None
        C[inc] += n
        tgt = C[inc]
        if selfwait and prev > 0:
            on(eng, lambda E_, f=f, s=S[es], p=prev, si=S[inc], m=n:
               f(E_)._wait_ge(s, p).then_inc(si, m))
        else:
            on(eng, lambda E_, f=f, si=S[inc], m=n: f(E_).then_inc(si, m))
        return tgt

    def dma(eng, sem, out, in_):
        prev = C[sem]
        C[sem] += 16
        tgt = C[sem]
        on(eng, lambda E_, o=out, i=in_, s=S[sem], p=prev:
           ((E_.dma_start(out=o, in_=i)._wait_ge(s, p) if p > 0
             else E_.dma_start(out=o, in_=i)).then_inc(s, 16)))
        return tgt

    def barrier():
        snap = {e: C[BSEM[e]] + 1 for e in ENGS}
        dsnap = {s: C[s] for s in DSEMS}
        for e in ENGS:
            do(e, lambda E_: E_.nop(), inc=BSEM[e])
            for o in ENGS:
                if o != e:
                    wait(e, BSEM[o], snap[o])
            for s in DSEMS:
                wait(e, s, dsnap[s])

    nc.const_aps.register(F32, EPS, ceps)

    li = dma("sync", "lc", identf, ident_f)
    do("vector", lambda E_: E_.memset(ceps, EPS), inc="dv")
    lc = dma("sync", "lc", tmpc[:, 0:128], ident_bd)
    wait("vector", "lc", lc)
    do("vector", lambda E_: E_.tensor_copy(identb, tmpc[:, 0:128]), inc="dv")
    lc = dma("sync", "lc", stage[0:16, 0:2048], indic_bd)
    wait("vector", "lc", lc)
    do("vector", lambda E_: E_.tensor_copy(indic, stage[0:16, 0:2048]), inc="dv")
    wait("sync", "dv", C["dv"])
    lc = dma("sync", "lc", tmpc[:, 0:128], ones_bd)
    wait("vector", "lc", lc)
    do("vector", lambda E_: E_.tensor_copy(ones128, tmpc[:, 0:128]), inc="dv")
    lc = dma("sync", "lc", stage[0:16, 0:128], expand_bd)
    wait("vector", "lc", lc)
    do("vector", lambda E_: E_.tensor_copy(expand, stage[0:16, 0:128]), inc="dv")
    wait("sync", "dv", C["dv"])
    ldA = dma("sync", "lc", A_sc.rearrange("p (c n) -> p c n", c=NC),
              A_log.rearrange("(c p) n -> p c n", p=128))
    wait("scalar", "lc", ldA)
    acA = do("scalar", lambda E_: E_.activation(out=A_sc, in_=A_sc, func=AF.Exp), inc="ac")
    wait("vector", "ac", acA)
    do("vector", lambda E_: E_.tensor_scalar_mul(A_sc, A_sc, -1.0), inc="dv")

    def load_cols(dst, src, scale=None):
        l = dma("sync", "lc", stage[0:32, 0:128], src.rearrange("(c p) -> c p", p=128))
        wait("tensor", "lc", l)
        wait("tensor", "dv", C["dv"])
        p = do("tensor", lambda E_: E_.transpose(out=psC[:, 0:32], in_=stage[0:32, 0:128],
                                                 identity=identf[0:32, 0:32]), inc="pe")
        wait("vector", "pe", p)
        if scale is None:
            do("vector", lambda E_, d=dst: E_.tensor_copy(d, psC[:, 0:32]), inc="dv")
        else:
            do("vector", lambda E_, d=dst, s=scale: E_.tensor_scalar_mul(d, psC[:, 0:32], s), inc="dv")
        wait("sync", "dv", C["dv"])

    if max_phase >= -2:
        load_cols(D_col, Dskip)
        load_cols(dtb_col, dt_proj_b)
        load_cols(bb_col, b_base, scale=0.5)

    if max_phase >= -2:
        v = do("vector", lambda E_: E_.memset(pD(0)[:, 0:1], 0.0), inc="dv")
        wait("gpsimd", "dv", C["dv"])
        l = dma("gpsimd", "lg", stage[0:1, 0:1], b_thr.rearrange("(a x) -> a x", a=1))
        wait("vector", "lg", l)
        v = do("vector", lambda E_: E_.tensor_copy(pD(0)[0:1, 0:1], stage[0:1, 0:1]), inc="dv")
        wait("tensor", "dv", v)
        p = do("tensor", lambda E_: E_.matmul(out=psC[:, 32:33], lhsT=ones128, rhs=pD(0)[:, 0:1],
                                              start=True, stop=True), inc="pe")
        wait("vector", "pe", p)
        do("vector", lambda E_: E_.tensor_copy(thrb, psC[:, 32:33]), inc="dv")

    stg_big = big[:, 8192:16384].bitcast(F32)

    def rows_to_bf(dst_bf_rows, src_dram, nrows, stg=None):
        stg = stg if stg is not None else stage
        ng = (nrows + 31) // 32
        for g in range(ng):
            r0, r1 = 32 * g, min(32 * (g + 1), nrows)
            l = dma("sync", "lc", stg[0:r1 - r0, 0:4096], src_dram[r0:r1, :])
            wait("vector", "lc", l)
            do("vector", lambda E_, g=g, r0=r0, r1=r1, dst=dst_bf_rows, stg=stg:
               E_.tensor_copy(dst(r0, r1), stg[0:r1 - r0, 0:4096]), inc="dv")
            wait("sync", "dv", C["dv"])

    def transpose_rows(src_rows_bf, ncols_rows, dst_fn):
        wait("tensor", "dv", C["dv"])
        idr = identb[0:ncols_rows, 0:ncols_rows]
        for c in range(NC):
            p = do("tensor", lambda E_, c=c, idr=idr: E_.transpose(
                out=psDb[:, 0:ncols_rows], in_=src_rows_bf(c), identity=idr), inc="pe")
            wait("vector", "pe", p)
            v = do("vector", lambda E_, c=c: E_.tensor_copy(dst_fn(c), psDb[:, 0:ncols_rows]), inc="dv")
            wait("tensor", "dv", v)

    def emit_small_weights():
        wait("sync", "dv", C["dv"])
        wait("sync", "pe", C["pe"])
        l = dma("sync", "lc", stg_big[0:8, 0:4096], W_gate)
        l = dma("sync", "lc", stg_big[8:9, 0:4096], W_thr.rearrange("(a d) -> a d", a=1))
        wait("vector", "lc", l)
        v = do("vector", lambda E_: E_.tensor_copy(big[0:9, 0:4096], stg_big[0:9, 0:4096]), inc="dv")
        transpose_rows(lambda c: big[0:9, bass.ts(c, 128)], 9,
                       lambda c: WgTb[:, c * 9:(c + 1) * 9])
        wait("sync", "dv", C["dv"])
        rows_to_bf(lambda r0, r1: big[r0:r1, 0:4096], A_w, 128, stg=stg_big)
        transpose_rows(lambda c: big[0:128, bass.ts(c, 128)], 128,
                       lambda c: AfT[:, bass.ts(c, 128)])

    if max_phase >= -1:
        emit_small_weights()

    if max_phase >= -1:
        rows_to_bf(lambda r0, r1: big[r0:r1, 0:4096], x_proj_w, 128)
    for g in (range(1) if max_phase >= -1 else ()):
        l = dma("sync", "lc", stage[0:32, 0:4096], x_proj_w[128:160, :])
        wait("vector", "lc", l)
        do("vector", lambda E_: E_.tensor_copy(big[0:32, 4096:8192], stage[0:32, 0:4096]), inc="dv")
        wait("sync", "dv", C["dv"])
    if max_phase >= -1:
        transpose_rows(lambda c: big[0:128, bass.ts(c, 128)], 128,
                       lambda c: xpT[:, c * 160:c * 160 + 128])
        wait("tensor", "dv", C["dv"])
        for c in range(NC):
            p = do("tensor", lambda E_, c=c: E_.transpose(
                out=psDb[:, 0:32], in_=big[0:32, 4096 + 128 * c:4096 + 128 * (c + 1)],
                identity=identb[0:32, 0:32]), inc="pe")
            wait("vector", "pe", p)
            v = do("vector", lambda E_, c=c: E_.tensor_copy(
                xpT[:, c * 160 + 128:(c + 1) * 160], psDb[:, 0:32]), inc="dv")
            wait("tensor", "dv", v)


    def emit_bft(upto=99):
        engs8 = ["sync", "gpsimd", "scalar"]
        sems8 = ["bl0", "bl1", "bl2", "bl3", "bl4", "bl5", "bl6", "bl7"]
        lds = []
        for e in range(E):
            eng = engs8[e % 3]
            wait(eng, "dv", C["dv"])
            wait(eng, "pe", C["pe"])
            l = None
            for c in range(NC):
                l = dma(eng, sems8[e],
                        stg_big[:, e * 512 + c * RK:e * 512 + (c + 1) * RK],
                        B_w[(e * D + c * 128):(e * D + (c + 1) * 128), :])
            lds.append(l)
        slots = [psC[0:16, 0:128], psC[0:16, 128:256], psD[0:16, 0:128], psD[0:16, 128:256]]
        bufs = [(stage2, "lg"), (stageB, "lg1")]
        pe_i = {}
        dv_i = {}
        st_dma = {0: 0, 1: 0}
        if upto < 1:
            return
        elim = upto if 2 < upto < 90 else (upto - 100 if upto >= 100 else E)
        wait("tensor", "dv", C["dv"])
        for e in range(min(E, elim)):
            wait("tensor", sems8[e], lds[e])
            for hf in range(2):
                bi = (e * 2 + hf) % 2
                buf, bsem = bufs[bi]
                wait("vector", bsem, st_dma[bi])
                for i in range(16):
                    g = e * 32 + hf * 16 + i
                    if (g - 4) in dv_i:
                        wait("tensor", "dv", dv_i[g - 4])
                    src = stg_big[:, e * 512 + hf * 256 + i * 16:e * 512 + hf * 256 + (i + 1) * 16]
                    pe_i[g] = do("tensor", lambda E_, src=src, sl=slots[i % 4]: E_.transpose(
                        out=sl, in_=src, identity=identf), inc="pe")
                    wait("vector", "pe", pe_i[g])
                    dv_i[g] = do("vector", lambda E_, i=i, sl=slots[i % 4], buf=buf: E_.tensor_scalar_mul(
                        buf[0:16, i * 128:(i + 1) * 128], sl, SCALING), inc="dv")
                if upto < 2:
                    continue
                wait("gpsimd", "dv", dv_i[e * 32 + hf * 16 + 15])
                st_dma[bi] = dma("gpsimd", bsem,
                                 BfT[e * 16:(e + 1) * 16, hf * 2048:(hf + 1) * 2048],
                                 buf[0:16, 0:2048])

    if max_phase >= 1:
        emit_bft()
        wait("scalar", "dv", C["dv"])
        dma("scalar", "se", sc_aft, AfT)
        dma("scalar", "se", sc_wgt, WgTb)
        wait("gpsimd", "lg", C["lg"])
        wait("gpsimd", "lg1", C["lg1"])
        dma("gpsimd", "lg", sc_bft, BfT)
    barrier()

    def ln_stats_tm(xt):
        do("vector", lambda E_, xt=xt: E_.reduce_sum(colt[:, 2:3], xt, axis=AX.X), inc="dv")
        do("scalar", lambda E_, xt=xt: E_.activation(out=pD(0), in_=xt[:, 0:2048],
                                                     func=AF.Square, accum_out=colt[:, 3:4]), inc="ac")
        a = do("scalar", lambda E_, xt=xt: E_.activation(out=pD(1), in_=xt[:, 2048:4096],
                                                         func=AF.Square, accum_out=colt[:, 4:5]), inc="ac")
        wait("vector", "ac", a)
        do("vector", lambda E_: E_.tensor_scalar_mul(colt[:, 2:3], colt[:, 2:3], 1.0 / D), inc="dv")
        do("vector", lambda E_: E_.tensor_add(colt[:, 3:4], colt[:, 3:4], colt[:, 4:5]), inc="dv")
        do("vector", lambda E_: E_.tensor_scalar_mul(colt[:, 3:4], colt[:, 3:4], 1.0 / D), inc="dv")
        do("vector", lambda E_: E_.tensor_mul(colt[:, 4:5], colt[:, 2:3], colt[:, 2:3]), inc="dv")
        v = do("vector", lambda E_: E_.tensor_sub(colt[:, 3:4], colt[:, 3:4], colt[:, 4:5]), inc="dv")
        wait("scalar", "dv", v)
        do("scalar", lambda E_: E_.activation(out=colt[:, 3:4], in_=colt[:, 3:4], func=AF.Ln, bias=EPS), inc="ac")
        a = do("scalar", lambda E_: E_.activation(out=colt[:, 0:1], in_=colt[:, 3:4], func=AF.Exp, scale=-0.5), inc="ac")
        wait("vector", "ac", a)
        do("vector", lambda E_: E_.tensor_mul(colt[:, 1:2], colt[:, 2:3], colt[:, 0:1]), inc="dv")
        return do("vector", lambda E_: E_.tensor_scalar_mul(colt[:, 1:2], colt[:, 1:2], -1.0), inc="dv")

    def emit_ln1_half(h):
        xtb = lambda k: wldA if k % 2 == 0 else wld_f
        tdone = {}
        ld = {}
        s0_cnt = {}

        def ln1_load(k):
            if k >= 8:
                return
            pe_, dv_, ac_ = tdone.get(k - 2, (C["pe"], C["dv"], C["ac"]))
            wait("sync", "pe", pe_)
            wait("sync", "dv", dv_)
            wait("sync", "ac", ac_)
            ld[k] = dma("sync", "lx", xtb(k),
                        x_in[(h * 8 + k) * 128:(h * 8 + k + 1) * 128, :])

        ln1_load(0)
        ln1_load(1)
        for k in range(8):
            tt = h * 8 + k
            xt = xtb(k)
            wait("vector", "lx", ld[k])
            wait("scalar", "lx", ld[k])
            dvst = ln_stats_tm(xt)
            wait("vector", "ac", C["ac"])
            wait("vector", "pe", C["pe"])
            va = do("vector", lambda E_, xt=xt: E_.tensor_scalar(
                ln_stg, xt, colt[:, 0:1], colt[:, 1:2], op0=ALU.mult, op1=ALU.add), inc="dv")
            wait("tensor", "dv", va)
            toff = k * 128
            for b8 in range(4):
                p = None
                for i in range(8):
                    c = b8 * 8 + i
                    dst = psDb if i < 4 else psCb
                    sl = (i % 4) * 256
                    p = do("tensor", lambda E_, c=c, dst=dst, sl=sl: E_.transpose(
                        out=dst[:, sl:sl + 128], in_=ln_stg[:, bass.ts(c, 128)],
                        identity=identb), inc="pe")
                wait("vector", "pe", p)
                vv = None
                for i in range(8):
                    c = b8 * 8 + i
                    src = psDb if i < 4 else psCb
                    sl = (i % 4) * 256
                    vv = do("vector", lambda E_, c=c, src=src, sl=sl, toff=toff: E_.tensor_copy(
                        xln[:, c * TH + toff:c * TH + toff + 128], src[:, sl:sl + 128]), inc="dv")
                wait("tensor", "dv", vv)
            for b8 in range(8):
                g = k * 8 + b8
                p = None
                for i in range(4):
                    c = b8 * 4 + i
                    dst = psC if i < 2 else psD
                    sl = (i % 2) * 256
                    p = do("tensor", lambda E_, c=c, xt=xt, dst=dst, sl=sl: E_.transpose(
                        out=dst[:, sl:sl + 128], in_=xt[:, bass.ts(c, 128)],
                        identity=identf), inc="pe")
                wait("vector", "pe", p)
                if (g - 2) in s0_cnt:
                    wait("vector", "s0", s0_cnt[g - 2])
                vv = None
                to = (b8 % 2) * 512
                for i in range(4):
                    dst = psC if i < 2 else psD
                    sl = (i % 2) * 256
                    vv = do("vector", lambda E_, i=i, dst=dst, sl=sl, to=to: E_.tensor_copy(
                        tmpc[:, to + i * 128:to + (i + 1) * 128], dst[:, sl:sl + 128]), inc="dv")
                wait("tensor", "dv", vv)
                wait("scalar", "dv", vv)
                s0_cnt[g] = dma("scalar", "s0",
                                sc_r0[b8 * 4 * 128:(b8 * 4 + 4) * 128, tt * 128:(tt + 1) * 128]
                                .rearrange("(c p) t -> p c t", p=128),
                                tmpc[:, to:to + 512].rearrange("p (c t) -> p c t", t=128))
            ln1_load(k + 2)
            tdone[k] = (C["pe"], C["dv"], C["ac"])

    def emit_gate_path(src_bf):
        wait("tensor", "dv", C["dv"])
        wait("tensor", "ac", C["ac"])
        plg = None
        for c in range(NC):
            for q in range(2):
                plg = do("tensor", lambda E_, c=c, q=q: E_.matmul(
                    out=psB[0:9, q * 512:(q + 1) * 512], lhsT=WgTb[:, c * 9:(c + 1) * 9],
                    rhs=src_bf[:, c * TH + q * 512:c * TH + (q + 1) * 512],
                    start=(c == 0), stop=(c == NC - 1)), inc="pe")
        ph = None
        for c in range(NC):
            for q in range(2):
                ph = do("tensor", lambda E_, c=c, q=q: E_.matmul(
                    out=psA[:, q * 512:(q + 1) * 512], lhsT=AfT[:, bass.ts(c, 128)],
                    rhs=src_bf[:, c * TH + q * 512:c * TH + (q + 1) * 512],
                    start=(c == 0), stop=(c == NC - 1)), inc="pe")
        wait("scalar", "pe", ph)
        ach = do("scalar", lambda E_: E_.activation(out=g_h, in_=psA[:, 0:1024], func=AF.Identity), inc="ac")
        wait("vector", "pe", plg)
        vl = do("vector", lambda E_: E_.tensor_copy(tmpc[0:9, 0:1024], psB[0:9, 0:1024]), inc="dv")
        wait("tensor", "dv", vl)
        for q in range(8):
            p = do("tensor", lambda E_, q=q: E_.transpose(
                out=psC[:, 0:9], in_=tmpc[0:9, q * 128:(q + 1) * 128],
                identity=identf[0:9, 0:9]), inc="pe")
            wait("vector", "pe", p)
            v = do("vector", lambda E_, q=q: E_.tensor_copy(poolC[:, q * 16:q * 16 + 9], psC[:, 0:9]), inc="dv")
            wait("tensor", "dv", v)
        for q in range(8):
            lg = poolC[:, q * 16:q * 16 + 8]
            th = poolC[:, q * 16 + 8:q * 16 + 9]
            do("vector", lambda E_, lg=lg: E_.reduce_max(colt[:, 5:6], lg, axis=AX.X), inc="dv")
            v = do("vector", lambda E_: E_.tensor_scalar_mul(colt[:, 5:6], colt[:, 5:6], -1.0), inc="dv")
            wait("scalar", "dv", v)
            do("scalar", lambda E_, lg=lg: E_.activation(out=lg, in_=lg, func=AF.Exp, bias=colt[:, 5:6]), inc="ac")
            a = do("scalar", lambda E_, th=th: E_.activation(out=th, in_=th, func=AF.Sigmoid, bias=thrb), inc="ac")
            wait("vector", "ac", a)
            do("vector", lambda E_, lg=lg: E_.reduce_sum(colt[:, 5:6], lg, axis=AX.X), inc="dv")
            do("vector", lambda E_: E_.reciprocal(colt[:, 6:7], colt[:, 5:6]), inc="dv")
            do("vector", lambda E_, th=th: E_.tensor_scalar_mul(th, th, MAXTHR), inc="dv")
            do("vector", lambda E_, lg=lg: E_.tensor_scalar_mul(lg, lg, colt[:, 6:7]), inc="dv")
            do("vector", lambda E_, lg=lg, th=th: E_.tensor_scalar(
                lg, lg, th, None, op0=ALU.subtract), inc="dv")
            do("vector", lambda E_, lg=lg: E_.tensor_scalar_max(lg, lg, 0.0), inc="dv")
            do("vector", lambda E_, lg=lg: E_.reduce_sum(colt[:, 5:6], lg, axis=AX.X), inc="dv")
            do("vector", lambda E_: E_.tensor_single_scalar(colt[:, 6:7], colt[:, 5:6], 0.0, op=ALU.is_equal), inc="dv")
            do("vector", lambda E_: E_.tensor_add(colt[:, 5:6], colt[:, 5:6], colt[:, 6:7]), inc="dv")
            do("vector", lambda E_: E_.reciprocal(colt[:, 6:7], colt[:, 5:6]), inc="dv")
            do("vector", lambda E_, lg=lg: E_.tensor_scalar_mul(lg, lg, colt[:, 6:7]), inc="dv")
        v = do("vector", lambda E_: E_.memset(stage2[0:16, 0:1024], 0.0), inc="dv")
        for q in range(8):
            v = do("vector", lambda E_, q=q: E_.tensor_copy(pD(1)[:, q * 16:q * 16 + 8],
                                                            poolC[:, q * 16:q * 16 + 8]), inc="dv")
        wait("tensor", "dv", v)
        for q in range(8):
            p = do("tensor", lambda E_, q=q: E_.transpose(
                out=psDb[0:8, 0:128], in_=pD(1)[:, q * 16:q * 16 + 8], identity=identb), inc="pe")
            wait("vector", "pe", p)
            v = do("vector", lambda E_, q=q: E_.tensor_copy(stage2[0:8, q * 128:(q + 1) * 128],
                                                            psDb[0:8, 0:128]), inc="dv")
            wait("tensor", "dv", v)
        pw = None
        for q in range(2):
            pw = do("tensor", lambda E_, q=q: E_.matmul(
                out=psB[:, q * 512:(q + 1) * 512], lhsT=expand,
                rhs=stage2[0:16, q * 512:(q + 1) * 512], start=True, stop=True), inc="pe")
        wait("vector", "pe", pw)
        wait("vector", "ac", ach)
        return (do("vector", lambda E_: E_.tensor_mul(g_hw, g_h, psB[:, 0:1024]), inc="dv"), ach)

    def emit_ffn_half(h, phase, resid_in, resid_out, resid_out_bf, sM=None, sR=None):
        if sM is not None:
            xb_dv = {}
            xld = {}
            xsem = lambda c: "lr" if c % 2 == 0 else "lrs"

            def xload(c):
                if c >= NC:
                    return
                if c >= 2:
                    wait("sync", "dv", xb_dv[c - 2])
                else:
                    wait("sync", "dv", C["dv"])
                xld[c] = dma("sync", xsem(c), poolC[:, (c % 2) * 1024:(c % 2 + 1) * 1024],
                             resid_in[c * 128:(c + 1) * 128, h * TH:(h + 1) * TH])

            xload(0)
            xload(1)
            for c in range(NC):
                rb = poolC[:, (c % 2) * 1024:(c % 2 + 1) * 1024]
                wait("vector", xsem(c), xld[c])
                do("vector", lambda E_, rb=rb, h=h: E_.tensor_sub(
                    tmpc[:, 0:1024], rb, sM[:, h * TH:(h + 1) * TH]), inc="dv")
                xb_dv[c] = do("vector", lambda E_, c=c, h=h: E_.tensor_mul(
                    xln[:, c * TH:(c + 1) * TH], tmpc[:, 0:1024], sR[:, h * TH:(h + 1) * TH]), inc="dv")
                xload(c + 2)
        chain = (phase == 1 and h == 0)
        wldb = lambda j: wld_f if j % 2 == 0 else wldA
        wcb = lambda j: wcast if j % 2 == 0 else wcast2
        wlsem = lambda j: "lw" if j % 2 == 0 else "lwg"
        wleng = lambda j: "sync" if j % 2 == 0 else "gpsimd"
        swsem = lambda j: "sw" if j % 2 == 0 else "s0"
        cast_ac = {}
        tr_pe = {}
        st_cnt = {}
        wch_ld = {}

        def wch_load(j):
            if j >= NC:
                return
            eng = wleng(j)
            if j >= 2:
                wait(eng, "ac", cast_ac[j - 2])
            else:
                wait(eng, "pe", C["pe"])
                wait(eng, "dv", C["dv"])
            wch_ld[j] = dma(eng, wlsem(j), wldb(j), W_base[j * 128:(j + 1) * 128, :])

        def wch_cast(j):
            if j >= NC:
                return
            wait("scalar", wlsem(j), wch_ld[j])
            if j >= 2:
                wait("scalar", "pe", tr_pe[j - 2])
            cast_ac[j] = do("scalar", lambda E_, j=j: E_.activation(
                out=wcb(j), in_=wldb(j), func=AF.Identity), inc="ac")

        if chain:
            wch_load(0)
            wch_load(1)
        hwv, ach_h = emit_gate_path(xln)
        if chain:
            wch_cast(0)
            wch_cast(1)
        evac_dv = {}
        psAh = lambda p: psA[:, p * 1024:(p + 1) * 1024]
        lwsem = lambda j: "lw" if j % 2 == 0 else "lw1"
        ldw = {}
        mm_done = {}
        evac_ac = {}

        def issue_wload(j):
            if j >= NC or chain:
                return
            if j >= 2:
                wait("sync", "pe", mm_done[j - 2])
            ldw[j] = dma("sync", lwsem(j), wtt[j % 2], sc_wt[j])

        issue_wload(0)
        issue_wload(1)
        for j in range(NC):
            wtbuf = wtt[j % 2]
            if chain:
                wait("tensor", "ac", cast_ac[j])
                for b8 in range(4):
                    p = None
                    for i in range(8):
                        c = b8 * 8 + i
                        dst = psDb if i < 4 else psCb
                        sl = (i % 4) * 256
                        p = do("tensor", lambda E_, c=c, dst=dst, sl=sl, j=j: E_.transpose(
                            out=dst[:, sl:sl + 128], in_=wcb(j)[:, bass.ts(c, 128)],
                            identity=identb), inc="pe")
                    if b8 == 3:
                        tr_pe[j] = p
                    wait("vector", "pe", p)
                    if b8 == 0 and j >= 2:
                        wait("vector", swsem(j), st_cnt[j - 2])
                    vv = None
                    for i in range(8):
                        c = b8 * 8 + i
                        srcp = psDb if i < 4 else psCb
                        sl = (i % 4) * 256
                        vv = do("vector", lambda E_, c=c, srcp=srcp, sl=sl, wtbuf=wtbuf: E_.tensor_copy(
                            wtbuf[:, bass.ts(c, 128)], srcp[:, sl:sl + 128]), inc="dv")
                    wait("tensor", "dv", vv)
                wait("scalar", "dv", vv)
                st_cnt[j] = dma("scalar", swsem(j), sc_wt[j], wtbuf)
            else:
                wait("tensor", lwsem(j), ldw[j])
            if j == 0:
                wait("tensor", "ac", ach_h)
            if j >= 2:
                wait("tensor", "ac", evac_ac[j - 2])
            for c in range(NC):
                do("tensor", lambda E_, c=c, j=j, wtbuf=wtbuf: E_.matmul(
                    out=psAh(j % 2)[:, 0:512], lhsT=wtbuf[:, bass.ts(c, 128)],
                    rhs=xln[:, c * TH:c * TH + 512], start=(c == 0), stop=False), inc="pe")
                do("tensor", lambda E_, c=c, j=j, wtbuf=wtbuf: E_.matmul(
                    out=psAh(j % 2)[:, 512:1024], lhsT=wtbuf[:, bass.ts(c, 128)],
                    rhs=xln[:, c * TH + 512:c * TH + 1024], start=(c == 0), stop=False), inc="pe")
            if j == 0:
                wait("tensor", "dv", hwv)
            do("tensor", lambda E_, j=j: E_.matmul(
                out=psAh(j % 2)[:, 0:512], lhsT=BfT[:, bass.ts(j, 128)],
                rhs=g_hw[:, 0:512], start=False, stop=True), inc="pe")
            pj = do("tensor", lambda E_, j=j: E_.matmul(
                out=psAh(j % 2)[:, 512:1024], lhsT=BfT[:, bass.ts(j, 128)],
                rhs=g_hw[:, 512:1024], start=False, stop=True), inc="pe")
            mm_done[j] = pj
            issue_wload(j + 2)
            if chain:
                wch_load(j + 2)
            rb = poolC[:, (j % 2) * 1024:(j % 2 + 1) * 1024]
            wait("sync", "dv", C["dv"] if j < 2 else evac_dv.get(j - 2, 0))
            lr = dma("sync", "lr", rb, resid_in[j * 128:(j + 1) * 128, h * TH:(h + 1) * TH])
            wait("scalar", "pe", pj)
            if j >= 1:
                wait("scalar", "dv", evac_dv[j - 1])
            aj = do("scalar", lambda E_, j=j: E_.activation(
                out=tmpc[:, 0:1024], in_=psAh(j % 2)[:, 0:1024], func=AF.Identity, scale=0.5,
                bias=bb_col[:, j:j + 1]), inc="ac")
            evac_ac[j] = aj
            if chain:
                wch_cast(j + 2)
            fout = pA(2)[:, (j % 2) * 1024:(j % 2 + 1) * 1024]
            sfs = "sf" if j % 2 == 0 else "sf1"
            wait("vector", "ac", aj)
            wait("vector", "lr", lr)
            wait("vector", sfs, C[sfs])
            vj = do("vector", lambda E_, rb=rb, fout=fout: E_.tensor_add(
                fout, tmpc[:, 0:1024], rb), inc="dv")
            evac_dv[j] = vj
            wait("scalar", "dv", vj)
            dma("scalar", sfs, resid_out[j * 128:(j + 1) * 128, h * TH:(h + 1) * TH], fout)
            if resid_out_bf is not None:
                wait("gpsimd", "dv", vj)
                wait("gpsimd", "sb_", C["sb_"])
                pk = do("gpsimd", lambda E_, fout=fout: E_.tensor_copy(pD(0)[:, 0:1024], fout), inc="po")
                wait("scalar", "po", pk)
                dma("scalar", "sb_", resid_out_bf[j * 128:(j + 1) * 128, h * TH:(h + 1) * TH], pD(0)[:, 0:1024])

    def emit_ln_stats_fm(resid_bf, dst_m, dst_r):
        sld = {}
        spe = {}
        sac = {}
        ssem = lambda c: "lr" if c % 2 == 0 else "lrs"

        def sload(c):
            if c >= NC:
                return
            wait("sync", "pe", spe.get(c - 2, C["pe"] if c < 2 else 0))
            wait("sync", "ac", sac.get(c - 2, C["ac"] if c < 2 else 0))
            sld[c] = dma("sync", ssem(c), pD(c % 2), resid_bf[c * 128:(c + 1) * 128, :])

        sload(0)
        sload(1)
        for c in range(NC):
            buf = pD(c % 2)
            wait("scalar", ssem(c), sld[c])
            if c >= 1:
                wait("scalar", "pe", spe[c - 1])
            a = do("scalar", lambda E_, buf=buf: E_.activation(out=poolF[:, 0:2048], in_=buf,
                                                               func=AF.Square), inc="ac")
            sac[c] = a
            wait("tensor", ssem(c), sld[c])
            wait("tensor", "ac", a)
            for q in range(4):
                do("tensor", lambda E_, c=c, q=q, buf=buf: E_.matmul(
                    out=psA[:, q * 512:(q + 1) * 512], lhsT=ones128,
                    rhs=buf[:, q * 512:(q + 1) * 512],
                    start=(c == 0), stop=(c == NC - 1)), inc="pe")
            for q in range(2):
                do("tensor", lambda E_, c=c, q=q: E_.matmul(
                    out=psB[:, q * 512:(q + 1) * 512], lhsT=ones128,
                    rhs=poolF[:, q * 512:(q + 1) * 512],
                    start=(c == 0), stop=(c == NC - 1)), inc="pe")
            do("tensor", lambda E_, c=c: E_.matmul(
                out=psC, lhsT=ones128, rhs=poolF[:, 1024:1536],
                start=(c == 0), stop=(c == NC - 1)), inc="pe")
            spe[c] = do("tensor", lambda E_, c=c: E_.matmul(
                out=psD, lhsT=ones128, rhs=poolF[:, 1536:2048],
                start=(c == 0), stop=(c == NC - 1)), inc="pe")
            sload(c + 2)
        pdone = C["pe"]
        wait("vector", "pe", pdone)
        sq = pA(2)
        do("vector", lambda E_: E_.tensor_scalar_mul(dst_m, psA, 1.0 / D), inc="dv")
        do("vector", lambda E_, sq=sq: E_.tensor_copy(sq[:, 0:1024], psB), inc="dv")
        do("vector", lambda E_, sq=sq: E_.tensor_copy(sq[:, 1024:1536], psC), inc="dv")
        v = do("vector", lambda E_, sq=sq: E_.tensor_copy(sq[:, 1536:2048], psD), inc="dv")
        wait("scalar", "dv", v)
        a = do("scalar", lambda E_: E_.activation(out=dst_r, in_=dst_m, func=AF.Square), inc="ac")
        wait("vector", "ac", a)
        v = do("vector", lambda E_, sq=sq: E_.scalar_tensor_tensor(
            dst_r, sq, 1.0 / D, dst_r, op0=ALU.mult, op1=ALU.subtract), inc="dv")
        wait("scalar", "dv", v)
        do("scalar", lambda E_: E_.activation(out=dst_r, in_=dst_r, func=AF.Ln, bias=EPS), inc="ac")
        do("scalar", lambda E_: E_.activation(out=dst_r, in_=dst_r, func=AF.Exp, scale=-0.5), inc="ac")

    for h in range(2):
        if max_phase >= 2 + h:
            emit_ln1_half(h)
        barrier()
        if max_phase >= 4 + h:
            emit_ffn_half(h, 1, sc_r0, sc_r1, sc_r1b)
        barrier()

    sM2, sR2 = pA(0), pA(1)
    if max_phase >= 6:
        emit_ln_stats_fm(sc_r1b, sM2, sR2)
    barrier()

    peF32 = poolE.bitcast(F32)[:, 2048:4096]
    for half in (range(2) if max_phase >= 7 else ()):
        p4ld = {}
        p4dv = {}
        p4pe = {}
        p4lg = {}
        if half == 0:
            rbv = lambda c: pC(0) if c % 2 == 0 else peF32
            lsem = lambda c: "lr" if c % 2 == 0 else "lrs"

            def p4load(c):
                if c >= NC:
                    return
                wait("sync", "dv", p4dv.get(c - 2, C["dv"] if c < 2 else 0))
                p4ld[c] = dma("sync", lsem(c), rbv(c), sc_r1[c * 128:(c + 1) * 128, :])
        else:
            lsem = lambda c: "lx" if c % 2 == 0 else "lxm"

            def p4load(c):
                if c >= NC:
                    return
                wait("sync", "pe", p4pe.get(c - 2, C["pe"] if c < 2 else 0))
                p4ld[c] = dma("sync", lsem(c), pD(c % 2), sc_xmb[c * 128:(c + 1) * 128, :])

        p4load(0)
        p4load(1)
        for c in range(NC):
            if half == 0:
                rb = rbv(c)
                wait("vector", lsem(c), p4ld[c])
                wait("vector", "lg" if c % 2 == 0 else "lg1", p4lg.get(c - 2, 0))
                if c >= 2:
                    wait("vector", "pe", p4pe[c - 2])
                do("vector", lambda E_, rb=rb: E_.tensor_sub(pA(2), rb, sM2), inc="dv")
                v = do("vector", lambda E_, c=c: E_.tensor_mul(pD(c % 2), pA(2), sR2), inc="dv")
                p4dv[c] = v
                wait("gpsimd", "dv", v)
                p4lg[c] = dma("gpsimd", "lg" if c % 2 == 0 else "lg1",
                              sc_xmb[c * 128:(c + 1) * 128, :], pD(c % 2))
                wait("tensor", "dv", v)
            else:
                wait("tensor", lsem(c), p4ld[c])
            src = pD(c % 2)[:, half * TH:(half + 1) * TH]
            pp = None
            for q in range(2):
                do("tensor", lambda E_, c=c, q=q, src=src: E_.matmul(
                    out=psA[:, q * 512:(q + 1) * 512], lhsT=xpT[:, c * 160:c * 160 + 128],
                    rhs=src[:, q * 512:(q + 1) * 512], start=(c == 0), stop=(c == NC - 1)), inc="pe")
                do("tensor", lambda E_, c=c, q=q, src=src: E_.matmul(
                    out=psB[0:16, q * 512:(q + 1) * 512], lhsT=xpT[:, c * 160 + 128:c * 160 + 144],
                    rhs=src[:, q * 512:(q + 1) * 512], start=(c == 0), stop=(c == NC - 1)), inc="pe")
                pp = do("tensor", lambda E_, c=c, q=q, src=src: E_.matmul(
                    out=psA[0:16, 1024 + q * 512:1024 + (q + 1) * 512],
                    lhsT=xpT[:, c * 160 + 144:c * 160 + 160],
                    rhs=src[:, q * 512:(q + 1) * 512], start=(c == 0), stop=(c == NC - 1)), inc="pe")
            p4pe[c] = pp
            p4load(c + 2)
        pxp = C["pe"]
        wait("vector", "pe", pxp)
        do("vector", lambda E_, half=half: E_.tensor_copy(dtin[:, half * TH:(half + 1) * TH],
                                                          psA[:, 0:1024]), inc="dv")
        do("vector", lambda E_, half=half: E_.tensor_copy(Bsb[:, half * TH:(half + 1) * TH],
                                                          psB[0:16, 0:1024]), inc="dv")
        do("vector", lambda E_, half=half: E_.tensor_copy(Csb[:, half * TH:(half + 1) * TH],
                                                          psA[0:16, 1024:2048]), inc="dv")
        barrier()

    for c in (range(NC) if max_phase >= 8 else ()):
        wait("sync", "dv", C["dv"])
        l = dma("sync", "lc", tmpc[:, 0:128], dt_proj_w[c * 128:(c + 1) * 128, :])
        wait("vector", "lc", l)
        wait("vector", "pe", C["pe"])
        v = do("vector", lambda E_: E_.tensor_copy(pD(0)[:, 0:128], tmpc[:, 0:128]), inc="dv")
        wait("tensor", "dv", v)
        p = do("tensor", lambda E_: E_.transpose(out=psDb[:, 0:128], in_=pD(0)[:, 0:128],
                                                 identity=identb), inc="pe")
        wait("vector", "pe", p)
        wait("vector", "lg", C["lg"])
        v = do("vector", lambda E_, c=c: E_.tensor_copy(pD(1)[:, 0:128], psDb[:, 0:128]), inc="dv")
        wait("gpsimd", "dv", v)
        dma("gpsimd", "lg", sc_dtp[c], pD(1)[:, 0:128])
    barrier()

    bst = poolE[:, 4096:6144]
    for n in (range(NS) if max_phase >= 9 else ()):
        for r in range(2):
            wait("tensor", "ac", C["ac"])
            wait("tensor", "dv", C["dv"])
            p = None
            for q in range(2):
                p = do("tensor", lambda E_, n=n, q=q, r=r: E_.matmul(
                    out=psB[:, q * 512:(q + 1) * 512], lhsT=indic[:, bass.ts(n, 128)],
                    rhs=Bsb[:, r * 1024 + q * 512:r * 1024 + (q + 1) * 512],
                    start=True, stop=True), inc="pe")
            wait("scalar", "pe", p)
            do("scalar", lambda E_, n=n, r=r: E_.copy(
                out=bst[:, r * 1024:(r + 1) * 1024], in_=psB), inc="ac")
        wait("vector", "ac", C["ac"])
        v = None
        for tau in range(16):
            v = do("vector", lambda E_, n=n, tau=tau: E_.tensor_mul(
                dgB(n, tau), identb, bst[:, tau * 128:(tau + 1) * 128]), inc="dv")
        for r in range(2):
            wait("tensor", "ac", C["ac"])
            p = None
            for q in range(2):
                p = do("tensor", lambda E_, n=n, q=q, r=r: E_.matmul(
                    out=psB[:, q * 512:(q + 1) * 512], lhsT=indic[:, bass.ts(n, 128)],
                    rhs=Csb[:, r * 1024 + q * 512:r * 1024 + (q + 1) * 512],
                    start=True, stop=True), inc="pe")
            wait("scalar", "pe", p)
            do("scalar", lambda E_, n=n, r=r: E_.copy(
                out=Cb_n(n)[:, r * 1024:(r + 1) * 1024], in_=psB), inc="ac")
    barrier()

    if scan_taps:
        sct = {nm: nc.dram_tensor(f"sct_{nm}", [2, 128, T], F32, kind="ExternalOutput").ap()
               for nm in ("dt", "dA", "h")}
    dt_f = pA(0)
    dA_b = (pA(1), pA(2))
    psAb = psA.bitcast(BF16)
    dbx2 = lambda th: psALL[:, th * 1024:(th + 1) * 1024]
    dty = psALL[:, 2048:4096]
    dtxT = lambda p: poolE[:, (4096 + p * 2048):(6144 + p * 2048)]
    dvhh = {}
    pyn = {}
    hc_po = {}
    stt_dv = {}
    se_cnt = {}
    pr2_po = {}
    lrs_cnt = {}
    yinit_ac = {}
    yinit_pe = {}
    pdt_cnt = {}
    pend = None

    def flush_y(pn, pnn, psid):
        wait("tensor", "po", pn)
        pyy = None
        for q in range(4):
            pyy = do("tensor", lambda E_, q=q, n=pnn: E_.matmul(
                out=dty[:, q * 512:(q + 1) * 512], lhsT=identb,
                rhs=cbuf(0)[:, q * 512:(q + 1) * 512],
                start=False, stop=(n == NS - 1)), inc="pe")
        pyn[psid] = pyy

    def emit_tail(cm):
        wait("vector", "lrs", lrs_cnt[cm])
        wait("vector", "pe", pyn[cm * NS + NS - 1])
        vr2 = do("vector", lambda E_: E_.scalar_tensor_tensor(
            pC(0), dty, 1.0, pC(0), op0=ALU.mult, op1=ALU.add), inc="dv")
        stt_dv[cm] = vr2
        wait("scalar", "dv", vr2)
        se_cnt[cm] = dma("scalar", "se", sc_r2[cm * 128:(cm + 1) * 128, :], pC(0))
        wait("gpsimd", "dv", vr2)
        wait("gpsimd", "sb2", C["sb2"])
        pr2 = do("gpsimd", lambda E_: E_.tensor_copy(r2stg, pC(0)), inc="po")
        pr2_po[cm] = pr2
        wait("scalar", "po", pr2)
        dma("scalar", "sb2", sc_r2b[cm * 128:(cm + 1) * 128, :], r2stg)
        if cm + 1 < NC:
            wait("sync", "se", se_cnt[cm])
            wait("sync", "po", pr2_po[cm])
            lrs_cnt[cm + 1] = dma("sync", "lrs", pC(0),
                                  sc_r1[(cm + 1) * 128:(cm + 2) * 128, :])

    if max_phase >= 10:
        lrs_cnt[0] = dma("sync", "lrs", pC(0), sc_r1[0:128, :])
    for c in (range(NC) if max_phase >= 10 else ()):
        dsem = "ldt0" if c % 2 == 0 else "ldt1"
        wait("sync", "pe", yinit_pe.get(c - 1, C["pe"]))
        lxm = dma("sync", "lxm", pD(1), sc_xmb[c * 128:(c + 1) * 128, :])
        if c >= 2:
            wait("sync", "pe", pdt_cnt[c - 2])
        ldt = dma("sync", dsem, dtpbuf[:, (c % 2) * 128:(c % 2 + 1) * 128], sc_dtp[c])
        acdt_h = {}
        dvdtx_h = {}
        vev_h = {}
        for hh in range(2):
            if (c * NS - 1, hh) in dvhh:
                wait("tensor", "dv", dvhh[(c * NS - 1, hh)])
            else:
                wait("tensor", "ac", C["ac"])
                wait("tensor", "dv", C["dv"])
            if hh == 0:
                wait("tensor", dsem, ldt)
            pdt = None
            for q in range(2):
                qq = hh * 2 + q
                pdt = do("tensor", lambda E_, c=c, qq=qq: E_.matmul(
                    out=psALL[:, qq * 512:(qq + 1) * 512],
                    lhsT=dtpbuf[:, (c % 2) * 128:(c % 2 + 1) * 128],
                    rhs=dtin[:, qq * 512:(qq + 1) * 512], start=True, stop=True), inc="pe")
            if hh == 1:
                pdt_cnt[c] = pdt
            wait("scalar", "pe", pdt)
            do("scalar", lambda E_, c=c, hh=hh: E_.activation(
                out=dt_f[:, hh * 1024:(hh + 1) * 1024],
                in_=psALL[:, hh * 1024:(hh + 1) * 1024],
                func=AF.Exp, bias=dtb_col[:, c:c + 1]), inc="ac")
            acdt_h[hh] = do("scalar", lambda E_, hh=hh: E_.activation(
                out=dt_f[:, hh * 1024:(hh + 1) * 1024],
                in_=dt_f[:, hh * 1024:(hh + 1) * 1024],
                func=AF.Ln, bias=1.0), inc="ac")
            wait("vector", "ac", acdt_h[hh])
            if hh == 0:
                wait("vector", "lxm", lxm)
            dvdtx_h[hh] = do("vector", lambda E_, hh=hh: E_.tensor_mul(
                pD(0)[:, hh * 1024:(hh + 1) * 1024],
                dt_f[:, hh * 1024:(hh + 1) * 1024],
                pD(1)[:, hh * 1024:(hh + 1) * 1024]), inc="dv")
            wait("tensor", "dv", dvdtx_h[hh])
            ptr = None
            for tau in range(hh * 8, hh * 8 + 8):
                ptr = do("tensor", lambda E_, tau=tau: E_.transpose(
                    out=psAb[:, tau * 128:(tau + 1) * 128], in_=pD(0)[:, bass.ts(tau, 128)],
                    identity=identb), inc="pe")
            wait("vector", "pe", ptr)
            vv = None
            for tau in range(hh * 8, hh * 8 + 8):
                vv = do("vector", lambda E_, c=c, tau=tau: E_.tensor_copy(
                    dtxT(c % 2)[:, bass.ts(tau, 128)], psAb[:, tau * 128:(tau + 1) * 128]), inc="dv")
            vev_h[hh] = vv
        wait("scalar", "dv", dvdtx_h[1])
        acdxm = do("scalar", lambda E_, c=c: E_.activation(out=pD(1), in_=pD(1), func=AF.Identity,
                                                           scale=D_col[:, c:c + 1]), inc="ac")
        yinit_ac[c] = acdxm
        if scan_taps and c < 2:
            dma("scalar", "stp", sct["dt"][c], dt_f)
        for n in range(NS):
            par = n % 2
            sid = c * NS + n
            if (sid - 2, 1) in dvhh:
                wait("scalar", "dv", dvhh[(sid - 2, 1)])
            acda = do("scalar", lambda E_, c=c, n=n, par=par: E_.activation(
                out=dA_b[par], in_=dt_f, func=AF.Exp,
                scale=A_sc[:, c * NS + n:c * NS + n + 1]), inc="ac")
            for th in range(2):
                if (sid - 1, th) in dvhh:
                    wait("tensor", "dv", dvhh[(sid - 1, th)])
                if n == 0 and th == 0:
                    wait("tensor", "dv", vev_h[1])
                pb = None
                for tt8 in range(8):
                    tau = th * 8 + tt8
                    pb = do("tensor", lambda E_, c=c, n=n, th=th, tau=tau: E_.matmul(
                        out=dbx2(th)[:, (tau % 8) * 128:((tau % 8) + 1) * 128],
                        lhsT=dtxT(c % 2)[:, bass.ts(tau, 128)],
                        rhs=dgB(n, tau), start=True, stop=True), inc="pe")
                if th == 1:
                    if n == 1:
                        wait("tensor", "ac", yinit_ac[c])
                        if (c - 1) in stt_dv:
                            wait("tensor", "dv", stt_dv[c - 1])
                        py = None
                        for q in range(4):
                            py = do("tensor", lambda E_, q=q: E_.matmul(
                                out=dty[:, q * 512:(q + 1) * 512], lhsT=identb,
                                rhs=pD(1)[:, q * 512:(q + 1) * 512], start=True, stop=False), inc="pe")
                        yinit_pe[c] = py
                    if pend is not None:
                        flush_y(*pend)
                        pend = None
                if th == 0:
                    wait("vector", "ac", acda)
                    if (sid - 2) in hc_po:
                        wait("vector", "po", hc_po[sid - 2])
                wait("vector", "pe", pb)
                ini = 0.0 if th == 0 else hbuf(par)[:, 1023:1024]
                dvhh[(sid, th)] = do("vector", lambda E_, par=par, th=th, ini=ini: E_.tensor_tensor_scan(
                    hbuf(par)[:, th * 1024:(th + 1) * 1024],
                    dA_b[par][:, th * 1024:(th + 1) * 1024],
                    dbx2(th), ini, op0=ALU.mult, op1=ALU.add), inc="dv")
            if n == 0 and c > 0:
                emit_tail(c - 1)
            if sid - 1 in pyn:
                wait("gpsimd", "pe", pyn[sid - 1])
            wait("gpsimd", "dv", dvhh[(sid, 1)])
            hc_po[sid] = do("gpsimd", lambda E_, n=n, par=par: E_.tensor_mul(
                cbuf(par), hbuf(par), Cb_n(n)), inc="po")
            pend = (hc_po[sid], n, sid)
            if scan_taps and c < 2 and n == 0:
                wait("scalar", "ac", acda)
                dma("scalar", "stp", sct["dA"][c], dA_b[par])
                vx = do("vector", lambda E_, par=par: E_.tensor_copy(tmpc[:, 0:1024], hbuf(par)[:, 0:1024]), inc="dv")
                wait("scalar", "dv", vx)
                dma("scalar", "stp", sct["h"][c][:, 0:1024], tmpc[:, 0:1024])
                wait("vector", "stp", C["stp"])
    if max_phase >= 10:
        if pend is not None:
            flush_y(*pend)
            pend = None
        emit_tail(NC - 1)
    barrier()

    sM3, sR3 = pA(0), pA(1)
    if max_phase >= 11:
        emit_ln_stats_fm(sc_r2b, sM3, sR3)
    barrier()

    if max_phase >= 12:
        l1 = dma("gpsimd", "lg", BfT, sc_bft)
        l2 = dma("scalar", "se", AfT, sc_aft)
        l3 = dma("scalar", "se", WgTb, sc_wgt)
    barrier()
    for h in range(2):
        if max_phase >= 13:
            emit_ffn_half(h, 2, sc_r2, sc_r3, None, sM=sM3, sR=sR3)
        barrier()

    p8pe = {}
    p8st = {}
    p8ld = {}
    stgv = lambda tt: poolA[:, 0:4096] if tt % 2 == 0 else stg_big
    xtv = lambda tt: (big[:, 0:8192] if tt % 2 == 0 else big[:, 16384:24576]).bitcast(F32)
    lsem8 = lambda tt: "lx" if tt % 2 == 0 else "lxm"
    ssem8 = lambda tt: "so" if tt % 2 == 0 else "stp"

    def p8load(tt):
        if tt >= NT:
            return
        wait("sync", "pe", p8pe.get(tt - 2, C["pe"] if tt < 2 else 0))
        p8ld[tt] = dma("sync", lsem8(tt), stgv(tt).rearrange("p (c t) -> p c t", t=128),
                       sc_r3[:, tt * 128:(tt + 1) * 128].rearrange("(c p) t -> p c t", p=128))

    if max_phase >= 14:
        p8load(0)
        p8load(1)
    for tt in (range(NT) if max_phase >= 14 else ()):
        stg = stgv(tt)
        xt = xtv(tt)
        wait("tensor", lsem8(tt), p8ld[tt])
        for b4 in range(8):
            p = None
            for i in range(4):
                c = b4 * 4 + i
                dst = psC if i < 2 else psD
                sl = (i % 2) * 256
                p = do("tensor", lambda E_, c=c, dst=dst, sl=sl, stg=stg: E_.transpose(
                    out=dst[:, sl:sl + 128], in_=stg[:, bass.ts(c, 128)],
                    identity=identf), inc="pe")
            if b4 == 7:
                p8pe[tt] = p
            wait("vector", "pe", p)
            if b4 == 0:
                wait("vector", ssem8(tt), p8st.get(tt - 2, 0))
            vv = None
            for i in range(4):
                c = b4 * 4 + i
                src = psC if i < 2 else psD
                sl = (i % 2) * 256
                vv = do("vector", lambda E_, c=c, src=src, sl=sl, xt=xt: E_.tensor_copy(
                    xt[:, bass.ts(c, 128)], src[:, sl:sl + 128]), inc="dv")
            wait("tensor", "dv", vv)
        wait("scalar", "dv", vv)
        p8st[tt] = dma("scalar", ssem8(tt), out_d[tt * 128:(tt + 1) * 128, :], xt)
        p8load(tt + 2)

    tapmap = {"r0": (sc_r0, False), "r1": (sc_r1, False), "r2": (sc_r2, False),
              "xmb": (sc_xmb, True)}
    for name in debug_taps:
        srcap, isbf = tapmap[name]
        tap = nc.dram_tensor(f"tap_{name}", [D, T], F32, kind="ExternalOutput").ap()
        for c in range(NC):
            wait("sync", "stp", C["stp"])
            if isbf:
                lc = dma("sync", "lx", pD(0), srcap[c * 128:(c + 1) * 128, :])
                wait("vector", "lx", lc)
                v = do("vector", lambda E_: E_.tensor_copy(pA(0), pD(0)), inc="dv")
            else:
                lc = dma("sync", "lx", pD(0).bitcast(F32)[:, 0:1024],
                         srcap[c * 128:(c + 1) * 128, 0:1024])
                lc2 = dma("sync", "lxm", pD(1).bitcast(F32)[:, 0:1024],
                          srcap[c * 128:(c + 1) * 128, 1024:2048])
                wait("vector", "lx", lc)
                wait("vector", "lxm", lc2)
                do("vector", lambda E_: E_.tensor_copy(pA(0)[:, 0:1024],
                                                       pD(0).bitcast(F32)[:, 0:1024]), inc="dv")
                v = do("vector", lambda E_: E_.tensor_copy(pA(0)[:, 1024:2048],
                                                           pD(1).bitcast(F32)[:, 0:1024]), inc="dv")
            wait("scalar", "dv", v)
            dma("scalar", "stp", tap[c * 128:(c + 1) * 128, :], pA(0))

    wait("gpsimd", "so", C["so"])
    wait("gpsimd", "stp", C["stp"])

    with nc.Block() as block:
        @block.sync
        def _(sync):
            for f in prog["sync"]:
                f(sync)

        @block.tensor
        def _(tensor):
            for f in prog["tensor"]:
                f(tensor)

        @block.vector
        def _(vector):
            for f in prog["vector"]:
                f(vector)

        @block.scalar
        def _(scalar):
            for f in prog["scalar"]:
                f(scalar)

        @block.gpsimd
        def _(gpsimd):
            for f in prog["gpsimd"]:
                f(gpsimd)

    return nc


def _consts():
    ident = np.eye(128, dtype=np.float32)
    ident_b = ident.copy()
    ones_b = np.ones((128, 128), dtype=np.float32)
    indic = np.zeros((16, 16 * 128), dtype=np.float32)
    for n in range(16):
        indic[n, n * 128:(n + 1) * 128] = 1.0
    expand = np.zeros((16, 128), dtype=np.float32)
    for e in range(E):
        expand[e, e * RK:(e + 1) * RK] = 1.0
    return ident, ident_b, ones_b, indic, expand


def kernel(L0_attn_x, L0_attn_A_log, L0_attn_D, L0_attn_x_proj_w, L0_attn_dt_proj_w,
           L0_attn_dt_proj_b, L0_ffn_W_base, L0_ffn_b_base, L0_ffn_W_gate, L0_ffn_W_thr,
           L0_ffn_b_thr, L0_ffn_A_w, L0_ffn_B_w, _debug_taps=(), _max_phase=99,
           _n_cores=8, _nc_cache={}):
    ident, ident_b, ones_b, indic, expand = _consts()
    x = np.ascontiguousarray(np.asarray(L0_attn_x, dtype=np.float32))
    cf = lambda a: np.ascontiguousarray(np.asarray(a, dtype=np.float32))
    shared = {
        "A_log": cf(L0_attn_A_log), "Dskip": cf(L0_attn_D),
        "x_proj_w": cf(L0_attn_x_proj_w), "dt_proj_w": cf(L0_attn_dt_proj_w),
        "dt_proj_b": cf(L0_attn_dt_proj_b), "W_base": cf(L0_ffn_W_base),
        "b_base": cf(L0_ffn_b_base), "W_gate": cf(L0_ffn_W_gate),
        "W_thr": cf(L0_ffn_W_thr), "b_thr": cf(L0_ffn_b_thr),
        "A_w": cf(L0_ffn_A_w).reshape(E * RK, D),
        "B_w": cf(L0_ffn_B_w).reshape(E * D, RK),
        "ident_f": ident, "ident_b": ident_b, "ones_b": ones_b,
        "indic_b": indic, "expand_b": expand,
    }
    key = (tuple(_debug_taps), _max_phase)
    if key not in _nc_cache:
        _nc_cache[key] = build_program(debug_taps=_debug_taps, max_phase=_max_phase)
    nc = _nc_cache[key]
    in_maps = [dict(shared, x=np.ascontiguousarray(x[b])) for b in range(_n_cores)]
    res = run_bass_kernel_spmd(nc, in_maps, core_ids=list(range(_n_cores)))
    out = np.stack([np.asarray(r["out"], dtype=np.float32) for r in res.results], axis=0)
    if _n_cores < 8:
        out = np.concatenate([out] + [np.zeros_like(out[:1])] * (8 - _n_cores), axis=0)
    if _debug_taps:
        taps = [{k: np.asarray(r[f"tap_{k}"]) for k in _debug_taps} for r in res.results]
        return out, taps
    return out


# version 16 (final)
# speedup vs baseline: 1.1887x; 1.1887x over previous
import numpy as np
import ml_dtypes
import concourse.bass as bass
import concourse.mybir as mybir
from concourse.bass_utils import run_bass_kernel_spmd

F32 = mybir.dt.float32
BF16 = mybir.dt.bfloat16
AF = mybir.ActivationFunctionType
ALU = mybir.AluOpType
AX = mybir.AxisListType
BF = ml_dtypes.bfloat16

T = 2048
D = 4096
NC = D // 128
NT = T // 128
NS = 16
DTR = 128
E = 8
RK = 16
TH = T // 2
EPS = 1e-5
MAXTHR = 0.125
SCALING = 2.0

ENGS = ("sync", "tensor", "vector", "scalar", "gpsimd")
ESEM = {"sync": "sy", "tensor": "pe", "vector": "dv", "scalar": "ac", "gpsimd": "po"}
BSEM = {"sync": "bsy", "tensor": "bpe", "vector": "bdv", "scalar": "bac", "gpsimd": "bpo"}
DSEMS = ("lc", "lx", "lw", "lw1", "lr", "lxm", "lrs", "ldt0", "ldt1", "lg",
         "s0", "sw", "sf", "sf1", "sb_", "se", "sb2", "so", "stp",
         "bl0", "bl1", "bl2", "bl3", "bl4", "bl5", "bl6", "bl7", "lg1", "lwg")


def build_program(debug_taps=(), max_phase=99, scan_taps=False):
    nc = bass.Bass("TRN2", target_bir_lowering=False, debug=False)

    di = lambda n, s: nc.dram_tensor(n, s, F32, kind="ExternalInput").ap()
    dib = lambda n, s: nc.dram_tensor(n, s, BF16, kind="ExternalInput").ap()
    x_in = di("x", [T, D])
    A_log = di("A_log", [D, NS])
    Dskip = di("Dskip", [D])
    x_proj_w = di("x_proj_w", [DTR + 2 * NS, D])
    dt_proj_w = di("dt_proj_w", [D, DTR])
    dt_proj_b = di("dt_proj_b", [D])
    W_base = di("W_base", [D, D])
    b_base = di("b_base", [D])
    W_gate = di("W_gate", [E, D])
    W_thr = di("W_thr", [D])
    b_thr = di("b_thr", [1])
    A_w = di("A_w", [E * RK, D])
    B_w = di("B_w", [E * D, RK])
    ident_f = di("ident_f", [128, 128])
    ident_bd = di("ident_b", [128, 128])
    ones_bd = di("ones_b", [128, 128])
    indic_bd = di("indic_b", [16, 16 * 128])
    expand_bd = di("expand_b", [16, 128])
    out_d = nc.dram_tensor("out", [T, D], F32, kind="ExternalOutput").ap()

    ds = lambda n, s, dt=F32: nc.dram_tensor(n, s, dt, kind="Internal").ap()
    sc_r0 = ds("sc_r0", [D, T])
    sc_r1 = ds("sc_r1", [D, T])
    sc_r1b = ds("sc_r1b", [D, T], BF16)
    sc_r2 = ds("sc_r2", [D, T])
    sc_r2b = ds("sc_r2b", [D, T], BF16)
    sc_r3 = ds("sc_r3", [D, T])
    sc_xmb = ds("sc_xmb", [D, T], BF16)
    sc_wt = ds("sc_wt", [NC, 128, D], BF16)
    sc_dtp = ds("sc_dtp", [NC, 128, 128], BF16)
    sc_bft = ds("sc_bft", [128, D], BF16)
    sc_aft = ds("sc_aft", [128, D], BF16)
    sc_wgt = ds("sc_wgt", [128, 288], BF16)

    sb = lambda n, s, dt=F32: nc.alloc_sbuf_tensor(n, s, dt).ap()
    identf = sb("identf", [128, 128])
    identb = sb("identb", [128, 128], BF16)
    ones128 = sb("ones128", [128, 128], BF16)
    indic = sb("indic", [16, 16 * 128], BF16)
    expand = sb("expand", [16, 128], BF16)
    A_sc = sb("A_sc", [128, NC * NS])
    D_col = sb("D_col", [128, NC])
    dtb_col = sb("dtb_col", [128, NC])
    bb_col = sb("bb_col", [128, NC])
    thrb = sb("thrb", [128, 1])
    dtin = sb("dtin", [128, T], BF16)
    dtpbuf = sb("dtpbuf", [128, 256], BF16)
    colt = sb("colt", [128, 8])
    ceps = sb("ceps", [128, 1])
    tmpc = sb("tmpc", [128, 1024])

    big = sb("big", [128, 65536], BF16)
    xln = big[:, 0:NC * TH]
    wld_f = big[:, 32768:40960].bitcast(F32)
    wtt = [big[:, 40960:45056], big[:, 45056:49152]]
    ln_stg = big[:, 40960:45056]
    g_hw = big[:, 49152:50176]
    g_h = big[:, 50176:51200]
    AfT = big[:, 51200:55296]
    xpT = big[:, 55296:60416]
    WgTb = big[:, 60416:60704]
    wcast = big[:, 60704:64800]
    dgB = lambda n, tau: big[:, n * T + tau * 128:n * T + (tau + 1) * 128]
    Cb_n = lambda n: big[:, 32768 + n * T:32768 + (n + 1) * T]

    poolA = sb("poolA", [128, 6144])
    pA = lambda i: poolA[:, i * 2048:(i + 1) * 2048]
    poolC = sb("poolC", [128, 2048])
    pC = lambda i: poolC[:, i * 2048:(i + 1) * 2048]
    poolD = sb("poolD", [128, 4096], BF16)
    pD = lambda i: poolD[:, i * 2048:(i + 1) * 2048]
    poolE = sb("poolE", [128, 8192], BF16)
    BfT = poolE[:, 0:4096]
    hbuf = lambda p: poolE[:, p * 2048:(p + 1) * 2048]
    xbuf = lambda p: poolE[:, (2 + p) * 2048:(3 + p) * 2048]
    Bsb = poolE[0:16, 0:2048]
    Csb = poolE[0:16, 2048:4096]
    poolF = sb("poolF", [128, 2048], BF16)
    r2stg = tmpc.bitcast(BF16)
    cbuf = lambda p: poolF[:, 0:2048]
    stage = poolA[0:32, 0:4096]
    stage2 = poolF[0:16, 0:2048]
    stageB = poolE[0:16, 4096:6144]
    wldA = poolA[:, 0:4096]
    wcast2 = poolE[:, 4096:8192]

    psALL = nc.alloc_psum_tensor("psALL", [128, 4096], F32).ap()
    psA = psALL[:, 0:2048]
    psB = psALL[:, 2048:3072]
    psC = psALL[:, 3072:3584]
    psD = psALL[:, 3584:4096]
    psDb = psD.bitcast(BF16)
    psCb = psC.bitcast(BF16)
    dbxps = lambda p: psALL[:, p * 2048:(p + 1) * 2048]

    S = {k: nc.alloc_semaphore(k) for k in
         ("pe", "dv", "ac", "po", "sy", "bpe", "bdv", "bac", "bpo", "bsy") + DSEMS}
    C = {k: 0 for k in S}
    prog = {e: [] for e in ENGS}

    def on(eng, f):
        prog[eng].append(f)

    def wait(eng, sem, val):
        if val > 0:
            on(eng, lambda E_, s=S[sem], v=val: E_.wait_ge(s, v))

    def do(eng, f, inc=None, n=1):
        es = ESEM[eng]
        prev = C[es]
        selfwait = (eng != "tensor")
        if inc is None:
            if selfwait and prev > 0:
                on(eng, lambda E_, f=f, s=S[es], p=prev: f(E_)._wait_ge(s, p))
            else:
                on(eng, f)
            return None
        C[inc] += n
        tgt = C[inc]
        if selfwait and prev > 0:
            on(eng, lambda E_, f=f, s=S[es], p=prev, si=S[inc], m=n:
               f(E_)._wait_ge(s, p).then_inc(si, m))
        else:
            on(eng, lambda E_, f=f, si=S[inc], m=n: f(E_).then_inc(si, m))
        return tgt

    def dma(eng, sem, out, in_):
        prev = C[sem]
        C[sem] += 16
        tgt = C[sem]
        on(eng, lambda E_, o=out, i=in_, s=S[sem], p=prev:
           ((E_.dma_start(out=o, in_=i)._wait_ge(s, p) if p > 0
             else E_.dma_start(out=o, in_=i)).then_inc(s, 16)))
        return tgt

    def barrier():
        snap = {e: C[BSEM[e]] + 1 for e in ENGS}
        dsnap = {s: C[s] for s in DSEMS}
        for e in ENGS:
            do(e, lambda E_: E_.nop(), inc=BSEM[e])
            for o in ENGS:
                if o != e:
                    wait(e, BSEM[o], snap[o])
            for s in DSEMS:
                wait(e, s, dsnap[s])

    nc.const_aps.register(F32, EPS, ceps)

    li = dma("sync", "lc", identf, ident_f)
    do("vector", lambda E_: E_.memset(ceps, EPS), inc="dv")
    lc = dma("sync", "lc", tmpc[:, 0:128], ident_bd)
    wait("vector", "lc", lc)
    do("vector", lambda E_: E_.tensor_copy(identb, tmpc[:, 0:128]), inc="dv")
    lc = dma("sync", "lc", stage[0:16, 0:2048], indic_bd)
    wait("vector", "lc", lc)
    do("vector", lambda E_: E_.tensor_copy(indic, stage[0:16, 0:2048]), inc="dv")
    wait("sync", "dv", C["dv"])
    lc = dma("sync", "lc", tmpc[:, 0:128], ones_bd)
    wait("vector", "lc", lc)
    do("vector", lambda E_: E_.tensor_copy(ones128, tmpc[:, 0:128]), inc="dv")
    lc = dma("sync", "lc", stage[0:16, 0:128], expand_bd)
    wait("vector", "lc", lc)
    do("vector", lambda E_: E_.tensor_copy(expand, stage[0:16, 0:128]), inc="dv")
    wait("sync", "dv", C["dv"])
    ldA = dma("sync", "lc", A_sc.rearrange("p (c n) -> p c n", c=NC),
              A_log.rearrange("(c p) n -> p c n", p=128))
    wait("scalar", "lc", ldA)
    acA = do("scalar", lambda E_: E_.activation(out=A_sc, in_=A_sc, func=AF.Exp), inc="ac")
    wait("vector", "ac", acA)
    do("vector", lambda E_: E_.tensor_scalar_mul(A_sc, A_sc, -1.0), inc="dv")

    def load_cols(dst, src, scale=None):
        l = dma("sync", "lc", stage[0:32, 0:128], src.rearrange("(c p) -> c p", p=128))
        wait("tensor", "lc", l)
        wait("tensor", "dv", C["dv"])
        p = do("tensor", lambda E_: E_.transpose(out=psC[:, 0:32], in_=stage[0:32, 0:128],
                                                 identity=identf[0:32, 0:32]), inc="pe")
        wait("vector", "pe", p)
        if scale is None:
            do("vector", lambda E_, d=dst: E_.tensor_copy(d, psC[:, 0:32]), inc="dv")
        else:
            do("vector", lambda E_, d=dst, s=scale: E_.tensor_scalar_mul(d, psC[:, 0:32], s), inc="dv")
        wait("sync", "dv", C["dv"])

    if max_phase >= -2:
        load_cols(D_col, Dskip)
        load_cols(dtb_col, dt_proj_b)
        load_cols(bb_col, b_base, scale=0.5)

    if max_phase >= -2:
        v = do("vector", lambda E_: E_.memset(pD(0)[:, 0:1], 0.0), inc="dv")
        wait("gpsimd", "dv", C["dv"])
        l = dma("gpsimd", "lg", stage[0:1, 0:1], b_thr.rearrange("(a x) -> a x", a=1))
        wait("vector", "lg", l)
        v = do("vector", lambda E_: E_.tensor_copy(pD(0)[0:1, 0:1], stage[0:1, 0:1]), inc="dv")
        wait("tensor", "dv", v)
        p = do("tensor", lambda E_: E_.matmul(out=psC[:, 32:33], lhsT=ones128, rhs=pD(0)[:, 0:1],
                                              start=True, stop=True), inc="pe")
        wait("vector", "pe", p)
        do("vector", lambda E_: E_.tensor_copy(thrb, psC[:, 32:33]), inc="dv")

    stg_big = big[:, 8192:16384].bitcast(F32)

    def rows_to_bf(dst_bf_rows, src_dram, nrows, stg=None):
        stg = stg if stg is not None else stage
        ng = (nrows + 31) // 32
        for g in range(ng):
            r0, r1 = 32 * g, min(32 * (g + 1), nrows)
            l = dma("sync", "lc", stg[0:r1 - r0, 0:4096], src_dram[r0:r1, :])
            wait("vector", "lc", l)
            do("vector", lambda E_, g=g, r0=r0, r1=r1, dst=dst_bf_rows, stg=stg:
               E_.tensor_copy(dst(r0, r1), stg[0:r1 - r0, 0:4096]), inc="dv")
            wait("sync", "dv", C["dv"])

    def transpose_rows(src_rows_bf, ncols_rows, dst_fn):
        wait("tensor", "dv", C["dv"])
        idr = identb[0:ncols_rows, 0:ncols_rows]
        for c in range(NC):
            p = do("tensor", lambda E_, c=c, idr=idr: E_.transpose(
                out=psDb[:, 0:ncols_rows], in_=src_rows_bf(c), identity=idr), inc="pe")
            wait("vector", "pe", p)
            v = do("vector", lambda E_, c=c: E_.tensor_copy(dst_fn(c), psDb[:, 0:ncols_rows]), inc="dv")
            wait("tensor", "dv", v)

    def emit_small_weights():
        wait("sync", "dv", C["dv"])
        wait("sync", "pe", C["pe"])
        l = dma("sync", "lc", stg_big[0:8, 0:4096], W_gate)
        l = dma("sync", "lc", stg_big[8:9, 0:4096], W_thr.rearrange("(a d) -> a d", a=1))
        wait("vector", "lc", l)
        v = do("vector", lambda E_: E_.tensor_copy(big[0:9, 0:4096], stg_big[0:9, 0:4096]), inc="dv")
        transpose_rows(lambda c: big[0:9, bass.ts(c, 128)], 9,
                       lambda c: WgTb[:, c * 9:(c + 1) * 9])
        wait("sync", "dv", C["dv"])
        rows_to_bf(lambda r0, r1: big[r0:r1, 0:4096], A_w, 128, stg=stg_big)
        transpose_rows(lambda c: big[0:128, bass.ts(c, 128)], 128,
                       lambda c: AfT[:, bass.ts(c, 128)])

    if max_phase >= -1:
        emit_small_weights()

    if max_phase >= -1:
        rows_to_bf(lambda r0, r1: big[r0:r1, 0:4096], x_proj_w, 128)
    for g in (range(1) if max_phase >= -1 else ()):
        l = dma("sync", "lc", stage[0:32, 0:4096], x_proj_w[128:160, :])
        wait("vector", "lc", l)
        do("vector", lambda E_: E_.tensor_copy(big[0:32, 4096:8192], stage[0:32, 0:4096]), inc="dv")
        wait("sync", "dv", C["dv"])
    if max_phase >= -1:
        transpose_rows(lambda c: big[0:128, bass.ts(c, 128)], 128,
                       lambda c: xpT[:, c * 160:c * 160 + 128])
        wait("tensor", "dv", C["dv"])
        for c in range(NC):
            p = do("tensor", lambda E_, c=c: E_.transpose(
                out=psDb[:, 0:32], in_=big[0:32, 4096 + 128 * c:4096 + 128 * (c + 1)],
                identity=identb[0:32, 0:32]), inc="pe")
            wait("vector", "pe", p)
            v = do("vector", lambda E_, c=c: E_.tensor_copy(
                xpT[:, c * 160 + 128:(c + 1) * 160], psDb[:, 0:32]), inc="dv")
            wait("tensor", "dv", v)


    def emit_bft(upto=99):
        engs8 = ["sync", "gpsimd", "scalar"]
        sems8 = ["bl0", "bl1", "bl2", "bl3", "bl4", "bl5", "bl6", "bl7"]
        lds = []
        for e in range(E):
            eng = engs8[e % 3]
            wait(eng, "dv", C["dv"])
            wait(eng, "pe", C["pe"])
            l = None
            for c in range(NC):
                l = dma(eng, sems8[e],
                        stg_big[:, e * 512 + c * RK:e * 512 + (c + 1) * RK],
                        B_w[(e * D + c * 128):(e * D + (c + 1) * 128), :])
            lds.append(l)
        slots = [psC[0:16, 0:128], psC[0:16, 128:256], psD[0:16, 0:128], psD[0:16, 128:256]]
        bufs = [(stage2, "lg"), (stageB, "lg1")]
        pe_i = {}
        dv_i = {}
        st_dma = {0: 0, 1: 0}
        if upto < 1:
            return
        elim = upto if 2 < upto < 90 else (upto - 100 if upto >= 100 else E)
        wait("tensor", "dv", C["dv"])
        for e in range(min(E, elim)):
            wait("tensor", sems8[e], lds[e])
            for hf in range(2):
                bi = (e * 2 + hf) % 2
                buf, bsem = bufs[bi]
                wait("vector", bsem, st_dma[bi])
                for i in range(16):
                    g = e * 32 + hf * 16 + i
                    if (g - 4) in dv_i:
                        wait("tensor", "dv", dv_i[g - 4])
                    src = stg_big[:, e * 512 + hf * 256 + i * 16:e * 512 + hf * 256 + (i + 1) * 16]
                    pe_i[g] = do("tensor", lambda E_, src=src, sl=slots[i % 4]: E_.transpose(
                        out=sl, in_=src, identity=identf), inc="pe")
                    wait("vector", "pe", pe_i[g])
                    dv_i[g] = do("vector", lambda E_, i=i, sl=slots[i % 4], buf=buf: E_.tensor_scalar_mul(
                        buf[0:16, i * 128:(i + 1) * 128], sl, SCALING), inc="dv")
                if upto < 2:
                    continue
                wait("gpsimd", "dv", dv_i[e * 32 + hf * 16 + 15])
                st_dma[bi] = dma("gpsimd", bsem,
                                 BfT[e * 16:(e + 1) * 16, hf * 2048:(hf + 1) * 2048],
                                 buf[0:16, 0:2048])

    if max_phase >= 1:
        emit_bft()
        wait("scalar", "dv", C["dv"])
        dma("scalar", "se", sc_aft, AfT)
        dma("scalar", "se", sc_wgt, WgTb)
        wait("gpsimd", "lg", C["lg"])
        wait("gpsimd", "lg1", C["lg1"])
        dma("gpsimd", "lg", sc_bft, BfT)
    barrier()

    def ln_stats_tm(xt):
        do("vector", lambda E_, xt=xt: E_.reduce_sum(colt[:, 2:3], xt, axis=AX.X), inc="dv")
        do("scalar", lambda E_, xt=xt: E_.activation(out=pD(0), in_=xt[:, 0:2048],
                                                     func=AF.Square, accum_out=colt[:, 3:4]), inc="ac")
        a = do("scalar", lambda E_, xt=xt: E_.activation(out=pD(1), in_=xt[:, 2048:4096],
                                                         func=AF.Square, accum_out=colt[:, 4:5]), inc="ac")
        wait("vector", "ac", a)
        do("vector", lambda E_: E_.tensor_scalar_mul(colt[:, 2:3], colt[:, 2:3], 1.0 / D), inc="dv")
        do("vector", lambda E_: E_.tensor_add(colt[:, 3:4], colt[:, 3:4], colt[:, 4:5]), inc="dv")
        do("vector", lambda E_: E_.tensor_scalar_mul(colt[:, 3:4], colt[:, 3:4], 1.0 / D), inc="dv")
        do("vector", lambda E_: E_.tensor_mul(colt[:, 4:5], colt[:, 2:3], colt[:, 2:3]), inc="dv")
        v = do("vector", lambda E_: E_.tensor_sub(colt[:, 3:4], colt[:, 3:4], colt[:, 4:5]), inc="dv")
        wait("scalar", "dv", v)
        do("scalar", lambda E_: E_.activation(out=colt[:, 3:4], in_=colt[:, 3:4], func=AF.Ln, bias=EPS), inc="ac")
        a = do("scalar", lambda E_: E_.activation(out=colt[:, 0:1], in_=colt[:, 3:4], func=AF.Exp, scale=-0.5), inc="ac")
        wait("vector", "ac", a)
        do("vector", lambda E_: E_.tensor_mul(colt[:, 1:2], colt[:, 2:3], colt[:, 0:1]), inc="dv")
        return do("vector", lambda E_: E_.tensor_scalar_mul(colt[:, 1:2], colt[:, 1:2], -1.0), inc="dv")

    def emit_ln1_half(h):
        xtb = lambda k: wldA if k % 2 == 0 else wld_f
        tdone = {}
        ld = {}
        s0_cnt = {}

        def ln1_load(k):
            if k >= 8:
                return
            pe_, dv_, ac_ = tdone.get(k - 2, (C["pe"], C["dv"], C["ac"]))
            wait("sync", "pe", pe_)
            wait("sync", "dv", dv_)
            wait("sync", "ac", ac_)
            ld[k] = dma("sync", "lx", xtb(k),
                        x_in[(h * 8 + k) * 128:(h * 8 + k + 1) * 128, :])

        ln1_load(0)
        ln1_load(1)
        for k in range(8):
            tt = h * 8 + k
            xt = xtb(k)
            wait("vector", "lx", ld[k])
            wait("scalar", "lx", ld[k])
            dvst = ln_stats_tm(xt)
            wait("vector", "ac", C["ac"])
            wait("vector", "pe", C["pe"])
            va = do("vector", lambda E_, xt=xt: E_.tensor_scalar(
                ln_stg, xt, colt[:, 0:1], colt[:, 1:2], op0=ALU.mult, op1=ALU.add), inc="dv")
            wait("tensor", "dv", va)
            toff = k * 128
            for b8 in range(4):
                p = None
                for i in range(8):
                    c = b8 * 8 + i
                    dst = psDb if i < 4 else psCb
                    sl = (i % 4) * 256
                    p = do("tensor", lambda E_, c=c, dst=dst, sl=sl: E_.transpose(
                        out=dst[:, sl:sl + 128], in_=ln_stg[:, bass.ts(c, 128)],
                        identity=identb), inc="pe")
                wait("vector", "pe", p)
                vv = None
                for i in range(8):
                    c = b8 * 8 + i
                    src = psDb if i < 4 else psCb
                    sl = (i % 4) * 256
                    vv = do("vector", lambda E_, c=c, src=src, sl=sl, toff=toff: E_.tensor_copy(
                        xln[:, c * TH + toff:c * TH + toff + 128], src[:, sl:sl + 128]), inc="dv")
                wait("tensor", "dv", vv)
            for b8 in range(8):
                g = k * 8 + b8
                p = None
                for i in range(4):
                    c = b8 * 4 + i
                    dst = psC if i < 2 else psD
                    sl = (i % 2) * 256
                    p = do("tensor", lambda E_, c=c, xt=xt, dst=dst, sl=sl: E_.transpose(
                        out=dst[:, sl:sl + 128], in_=xt[:, bass.ts(c, 128)],
                        identity=identf), inc="pe")
                wait("vector", "pe", p)
                if (g - 2) in s0_cnt:
                    wait("vector", "s0", s0_cnt[g - 2])
                vv = None
                to = (b8 % 2) * 512
                for i in range(4):
                    dst = psC if i < 2 else psD
                    sl = (i % 2) * 256
                    vv = do("vector", lambda E_, i=i, dst=dst, sl=sl, to=to: E_.tensor_copy(
                        tmpc[:, to + i * 128:to + (i + 1) * 128], dst[:, sl:sl + 128]), inc="dv")
                wait("tensor", "dv", vv)
                wait("scalar", "dv", vv)
                s0_cnt[g] = dma("scalar", "s0",
                                sc_r0[b8 * 4 * 128:(b8 * 4 + 4) * 128, tt * 128:(tt + 1) * 128]
                                .rearrange("(c p) t -> p c t", p=128),
                                tmpc[:, to:to + 512].rearrange("p (c t) -> p c t", t=128))
            ln1_load(k + 2)
            tdone[k] = (C["pe"], C["dv"], C["ac"])

    def emit_gate_path(src_bf):
        wait("tensor", "dv", C["dv"])
        wait("tensor", "ac", C["ac"])
        plg = None
        for c in range(NC):
            for q in range(2):
                plg = do("tensor", lambda E_, c=c, q=q: E_.matmul(
                    out=psB[0:9, q * 512:(q + 1) * 512], lhsT=WgTb[:, c * 9:(c + 1) * 9],
                    rhs=src_bf[:, c * TH + q * 512:c * TH + (q + 1) * 512],
                    start=(c == 0), stop=(c == NC - 1)), inc="pe")
        ph = None
        for c in range(NC):
            for q in range(2):
                ph = do("tensor", lambda E_, c=c, q=q: E_.matmul(
                    out=psA[:, q * 512:(q + 1) * 512], lhsT=AfT[:, bass.ts(c, 128)],
                    rhs=src_bf[:, c * TH + q * 512:c * TH + (q + 1) * 512],
                    start=(c == 0), stop=(c == NC - 1)), inc="pe")
        wait("scalar", "pe", ph)
        ach = do("scalar", lambda E_: E_.activation(out=g_h, in_=psA[:, 0:1024], func=AF.Identity), inc="ac")
        wait("vector", "pe", plg)
        vl = do("vector", lambda E_: E_.tensor_copy(tmpc[0:9, 0:1024], psB[0:9, 0:1024]), inc="dv")
        wait("tensor", "dv", vl)
        for q in range(8):
            p = do("tensor", lambda E_, q=q: E_.transpose(
                out=psC[:, 0:9], in_=tmpc[0:9, q * 128:(q + 1) * 128],
                identity=identf[0:9, 0:9]), inc="pe")
            wait("vector", "pe", p)
            v = do("vector", lambda E_, q=q: E_.tensor_copy(poolC[:, q * 16:q * 16 + 9], psC[:, 0:9]), inc="dv")
            wait("tensor", "dv", v)
        for q in range(8):
            lg = poolC[:, q * 16:q * 16 + 8]
            th = poolC[:, q * 16 + 8:q * 16 + 9]
            do("vector", lambda E_, lg=lg: E_.reduce_max(colt[:, 5:6], lg, axis=AX.X), inc="dv")
            v = do("vector", lambda E_: E_.tensor_scalar_mul(colt[:, 5:6], colt[:, 5:6], -1.0), inc="dv")
            wait("scalar", "dv", v)
            do("scalar", lambda E_, lg=lg: E_.activation(out=lg, in_=lg, func=AF.Exp, bias=colt[:, 5:6]), inc="ac")
            a = do("scalar", lambda E_, th=th: E_.activation(out=th, in_=th, func=AF.Sigmoid, bias=thrb), inc="ac")
            wait("vector", "ac", a)
            do("vector", lambda E_, lg=lg: E_.reduce_sum(colt[:, 5:6], lg, axis=AX.X), inc="dv")
            do("vector", lambda E_: E_.reciprocal(colt[:, 6:7], colt[:, 5:6]), inc="dv")
            do("vector", lambda E_, th=th: E_.tensor_scalar_mul(th, th, MAXTHR), inc="dv")
            do("vector", lambda E_, lg=lg: E_.tensor_scalar_mul(lg, lg, colt[:, 6:7]), inc="dv")
            do("vector", lambda E_, lg=lg, th=th: E_.tensor_scalar(
                lg, lg, th, None, op0=ALU.subtract), inc="dv")
            do("vector", lambda E_, lg=lg: E_.tensor_scalar_max(lg, lg, 0.0), inc="dv")
            do("vector", lambda E_, lg=lg: E_.reduce_sum(colt[:, 5:6], lg, axis=AX.X), inc="dv")
            do("vector", lambda E_: E_.tensor_single_scalar(colt[:, 6:7], colt[:, 5:6], 0.0, op=ALU.is_equal), inc="dv")
            do("vector", lambda E_: E_.tensor_add(colt[:, 5:6], colt[:, 5:6], colt[:, 6:7]), inc="dv")
            do("vector", lambda E_: E_.reciprocal(colt[:, 6:7], colt[:, 5:6]), inc="dv")
            do("vector", lambda E_, lg=lg: E_.tensor_scalar_mul(lg, lg, colt[:, 6:7]), inc="dv")
        v = do("vector", lambda E_: E_.memset(stage2[0:16, 0:1024], 0.0), inc="dv")
        for q in range(8):
            v = do("vector", lambda E_, q=q: E_.tensor_copy(pD(1)[:, q * 16:q * 16 + 8],
                                                            poolC[:, q * 16:q * 16 + 8]), inc="dv")
        wait("tensor", "dv", v)
        for q in range(8):
            p = do("tensor", lambda E_, q=q: E_.transpose(
                out=psDb[0:8, 0:128], in_=pD(1)[:, q * 16:q * 16 + 8], identity=identb), inc="pe")
            wait("vector", "pe", p)
            v = do("vector", lambda E_, q=q: E_.tensor_copy(stage2[0:8, q * 128:(q + 1) * 128],
                                                            psDb[0:8, 0:128]), inc="dv")
            wait("tensor", "dv", v)
        pw = None
        for q in range(2):
            pw = do("tensor", lambda E_, q=q: E_.matmul(
                out=psB[:, q * 512:(q + 1) * 512], lhsT=expand,
                rhs=stage2[0:16, q * 512:(q + 1) * 512], start=True, stop=True), inc="pe")
        wait("vector", "pe", pw)
        wait("vector", "ac", ach)
        return (do("vector", lambda E_: E_.tensor_mul(g_hw, g_h, psB[:, 0:1024]), inc="dv"), ach)

    def emit_ffn_half(h, phase, resid_in, resid_out, resid_out_bf, sM=None, sR=None):
        if sM is not None:
            xb_dv = {}
            xld = {}
            xsem = lambda c: "lr" if c % 2 == 0 else "lrs"

            def xload(c):
                if c >= NC:
                    return
                if c >= 2:
                    wait("sync", "dv", xb_dv[c - 2])
                else:
                    wait("sync", "dv", C["dv"])
                xld[c] = dma("sync", xsem(c), poolC[:, (c % 2) * 1024:(c % 2 + 1) * 1024],
                             resid_in[c * 128:(c + 1) * 128, h * TH:(h + 1) * TH])

            xload(0)
            xload(1)
            for c in range(NC):
                rb = poolC[:, (c % 2) * 1024:(c % 2 + 1) * 1024]
                wait("vector", xsem(c), xld[c])
                do("vector", lambda E_, rb=rb, h=h: E_.tensor_sub(
                    tmpc[:, 0:1024], rb, sM[:, h * TH:(h + 1) * TH]), inc="dv")
                xb_dv[c] = do("vector", lambda E_, c=c, h=h: E_.tensor_mul(
                    xln[:, c * TH:(c + 1) * TH], tmpc[:, 0:1024], sR[:, h * TH:(h + 1) * TH]), inc="dv")
                xload(c + 2)
        chain = (phase == 1 and h == 0)
        wldb = lambda j: wld_f if j % 2 == 0 else wldA
        wcb = lambda j: wcast if j % 2 == 0 else wcast2
        wlsem = lambda j: "lw" if j % 2 == 0 else "lwg"
        wleng = lambda j: "sync" if j % 2 == 0 else "gpsimd"
        swsem = lambda j: "sw" if j % 2 == 0 else "s0"
        cast_ac = {}
        tr_pe = {}
        st_cnt = {}
        wch_ld = {}

        def wch_load(j):
            if j >= NC:
                return
            eng = wleng(j)
            if j >= 2:
                wait(eng, "ac", cast_ac[j - 2])
            else:
                wait(eng, "pe", C["pe"])
                wait(eng, "dv", C["dv"])
            wch_ld[j] = dma(eng, wlsem(j), wldb(j), W_base[j * 128:(j + 1) * 128, :])

        def wch_cast(j):
            if j >= NC:
                return
            wait("scalar", wlsem(j), wch_ld[j])
            if j >= 2:
                wait("scalar", "pe", tr_pe[j - 2])
            cast_ac[j] = do("scalar", lambda E_, j=j: E_.activation(
                out=wcb(j), in_=wldb(j), func=AF.Identity), inc="ac")

        if chain:
            wch_load(0)
            wch_load(1)
        hwv, ach_h = emit_gate_path(xln)
        if chain:
            wch_cast(0)
            wch_cast(1)
        evac_dv = {}
        psAh = lambda p: psA[:, p * 1024:(p + 1) * 1024]
        lwsem = lambda j: "lw" if j % 2 == 0 else "lw1"
        ldw = {}
        mm_done = {}
        evac_ac = {}

        def issue_wload(j):
            if j >= NC or chain:
                return
            if j >= 2:
                wait("sync", "pe", mm_done[j - 2])
            ldw[j] = dma("sync", lwsem(j), wtt[j % 2], sc_wt[j])

        issue_wload(0)
        issue_wload(1)
        for j in range(NC):
            wtbuf = wtt[j % 2]
            if chain:
                wait("tensor", "ac", cast_ac[j])
                for b8 in range(4):
                    p = None
                    for i in range(8):
                        c = b8 * 8 + i
                        dst = psDb if i < 4 else psCb
                        sl = (i % 4) * 256
                        p = do("tensor", lambda E_, c=c, dst=dst, sl=sl, j=j: E_.transpose(
                            out=dst[:, sl:sl + 128], in_=wcb(j)[:, bass.ts(c, 128)],
                            identity=identb), inc="pe")
                    if b8 == 3:
                        tr_pe[j] = p
                    wait("vector", "pe", p)
                    if b8 == 0 and j >= 2:
                        wait("vector", swsem(j), st_cnt[j - 2])
                    vv = None
                    for i in range(8):
                        c = b8 * 8 + i
                        srcp = psDb if i < 4 else psCb
                        sl = (i % 4) * 256
                        vv = do("vector", lambda E_, c=c, srcp=srcp, sl=sl, wtbuf=wtbuf: E_.tensor_copy(
                            wtbuf[:, bass.ts(c, 128)], srcp[:, sl:sl + 128]), inc="dv")
                    wait("tensor", "dv", vv)
                wait("scalar", "dv", vv)
                st_cnt[j] = dma("scalar", swsem(j), sc_wt[j], wtbuf)
            else:
                wait("tensor", lwsem(j), ldw[j])
            if j == 0:
                wait("tensor", "ac", ach_h)
            if j >= 2:
                wait("tensor", "ac", evac_ac[j - 2])
            for c in range(NC):
                do("tensor", lambda E_, c=c, j=j, wtbuf=wtbuf: E_.matmul(
                    out=psAh(j % 2)[:, 0:512], lhsT=wtbuf[:, bass.ts(c, 128)],
                    rhs=xln[:, c * TH:c * TH + 512], start=(c == 0), stop=False), inc="pe")
                do("tensor", lambda E_, c=c, j=j, wtbuf=wtbuf: E_.matmul(
                    out=psAh(j % 2)[:, 512:1024], lhsT=wtbuf[:, bass.ts(c, 128)],
                    rhs=xln[:, c * TH + 512:c * TH + 1024], start=(c == 0), stop=False), inc="pe")
            if j == 0:
                wait("tensor", "dv", hwv)
            do("tensor", lambda E_, j=j: E_.matmul(
                out=psAh(j % 2)[:, 0:512], lhsT=BfT[:, bass.ts(j, 128)],
                rhs=g_hw[:, 0:512], start=False, stop=True), inc="pe")
            pj = do("tensor", lambda E_, j=j: E_.matmul(
                out=psAh(j % 2)[:, 512:1024], lhsT=BfT[:, bass.ts(j, 128)],
                rhs=g_hw[:, 512:1024], start=False, stop=True), inc="pe")
            mm_done[j] = pj
            issue_wload(j + 2)
            if chain:
                wch_load(j + 2)
            rb = poolC[:, (j % 2) * 1024:(j % 2 + 1) * 1024]
            wait("sync", "dv", C["dv"] if j < 2 else evac_dv.get(j - 2, 0))
            lr = dma("sync", "lr", rb, resid_in[j * 128:(j + 1) * 128, h * TH:(h + 1) * TH])
            wait("scalar", "pe", pj)
            if j >= 1:
                wait("scalar", "dv", evac_dv[j - 1])
            aj = do("scalar", lambda E_, j=j: E_.activation(
                out=tmpc[:, 0:1024], in_=psAh(j % 2)[:, 0:1024], func=AF.Identity, scale=0.5,
                bias=bb_col[:, j:j + 1]), inc="ac")
            evac_ac[j] = aj
            if chain:
                wch_cast(j + 2)
            fout = pA(2)[:, (j % 2) * 1024:(j % 2 + 1) * 1024]
            sfs = "sf" if j % 2 == 0 else "sf1"
            wait("vector", "ac", aj)
            wait("vector", "lr", lr)
            wait("vector", sfs, C[sfs])
            vj = do("vector", lambda E_, rb=rb, fout=fout: E_.tensor_add(
                fout, tmpc[:, 0:1024], rb), inc="dv")
            evac_dv[j] = vj
            wait("scalar", "dv", vj)
            dma("scalar", sfs, resid_out[j * 128:(j + 1) * 128, h * TH:(h + 1) * TH], fout)
            if resid_out_bf is not None:
                wait("gpsimd", "dv", vj)
                wait("gpsimd", "sb_", C["sb_"])
                pk = do("gpsimd", lambda E_, fout=fout: E_.tensor_copy(pD(0)[:, 0:1024], fout), inc="po")
                wait("scalar", "po", pk)
                dma("scalar", "sb_", resid_out_bf[j * 128:(j + 1) * 128, h * TH:(h + 1) * TH], pD(0)[:, 0:1024])

    def emit_ln_stats_fm(resid_bf, dst_m, dst_r):
        sld = {}
        spe = {}
        sac = {}
        ssem = lambda c: "lr" if c % 2 == 0 else "lrs"

        def sload(c):
            if c >= NC:
                return
            wait("sync", "pe", spe.get(c - 2, C["pe"] if c < 2 else 0))
            wait("sync", "ac", sac.get(c - 2, C["ac"] if c < 2 else 0))
            sld[c] = dma("sync", ssem(c), pD(c % 2), resid_bf[c * 128:(c + 1) * 128, :])

        sload(0)
        sload(1)
        for c in range(NC):
            buf = pD(c % 2)
            wait("scalar", ssem(c), sld[c])
            if c >= 1:
                wait("scalar", "pe", spe[c - 1])
            a = do("scalar", lambda E_, buf=buf: E_.activation(out=poolF[:, 0:2048], in_=buf,
                                                               func=AF.Square), inc="ac")
            sac[c] = a
            wait("tensor", ssem(c), sld[c])
            wait("tensor", "ac", a)
            for q in range(4):
                do("tensor", lambda E_, c=c, q=q, buf=buf: E_.matmul(
                    out=psA[:, q * 512:(q + 1) * 512], lhsT=ones128,
                    rhs=buf[:, q * 512:(q + 1) * 512],
                    start=(c == 0), stop=(c == NC - 1)), inc="pe")
            for q in range(2):
                do("tensor", lambda E_, c=c, q=q: E_.matmul(
                    out=psB[:, q * 512:(q + 1) * 512], lhsT=ones128,
                    rhs=poolF[:, q * 512:(q + 1) * 512],
                    start=(c == 0), stop=(c == NC - 1)), inc="pe")
            do("tensor", lambda E_, c=c: E_.matmul(
                out=psC, lhsT=ones128, rhs=poolF[:, 1024:1536],
                start=(c == 0), stop=(c == NC - 1)), inc="pe")
            spe[c] = do("tensor", lambda E_, c=c: E_.matmul(
                out=psD, lhsT=ones128, rhs=poolF[:, 1536:2048],
                start=(c == 0), stop=(c == NC - 1)), inc="pe")
            sload(c + 2)
        pdone = C["pe"]
        wait("vector", "pe", pdone)
        sq = pA(2)
        do("vector", lambda E_: E_.tensor_scalar_mul(dst_m, psA, 1.0 / D), inc="dv")
        do("vector", lambda E_, sq=sq: E_.tensor_copy(sq[:, 0:1024], psB), inc="dv")
        do("vector", lambda E_, sq=sq: E_.tensor_copy(sq[:, 1024:1536], psC), inc="dv")
        v = do("vector", lambda E_, sq=sq: E_.tensor_copy(sq[:, 1536:2048], psD), inc="dv")
        wait("scalar", "dv", v)
        a = do("scalar", lambda E_: E_.activation(out=dst_r, in_=dst_m, func=AF.Square), inc="ac")
        wait("vector", "ac", a)
        v = do("vector", lambda E_, sq=sq: E_.scalar_tensor_tensor(
            dst_r, sq, 1.0 / D, dst_r, op0=ALU.mult, op1=ALU.subtract), inc="dv")
        wait("scalar", "dv", v)
        do("scalar", lambda E_: E_.activation(out=dst_r, in_=dst_r, func=AF.Ln, bias=EPS), inc="ac")
        do("scalar", lambda E_: E_.activation(out=dst_r, in_=dst_r, func=AF.Exp, scale=-0.5), inc="ac")

    for h in range(2):
        if max_phase >= 2 + h:
            emit_ln1_half(h)
        barrier()
        if max_phase >= 4 + h:
            emit_ffn_half(h, 1, sc_r0, sc_r1, sc_r1b)
        barrier()

    sM2, sR2 = pA(0), pA(1)
    if max_phase >= 6:
        emit_ln_stats_fm(sc_r1b, sM2, sR2)
    barrier()

    peF32 = poolE.bitcast(F32)[:, 2048:4096]
    for half in (range(2) if max_phase >= 7 else ()):
        p4ld = {}
        p4dv = {}
        p4pe = {}
        p4lg = {}
        if half == 0:
            rbv = lambda c: pC(0) if c % 2 == 0 else peF32
            lsem = lambda c: "lr" if c % 2 == 0 else "lrs"

            def p4load(c):
                if c >= NC:
                    return
                wait("sync", "dv", p4dv.get(c - 2, C["dv"] if c < 2 else 0))
                p4ld[c] = dma("sync", lsem(c), rbv(c), sc_r1[c * 128:(c + 1) * 128, :])
        else:
            lsem = lambda c: "lx" if c % 2 == 0 else "lxm"

            def p4load(c):
                if c >= NC:
                    return
                wait("sync", "pe", p4pe.get(c - 2, C["pe"] if c < 2 else 0))
                p4ld[c] = dma("sync", lsem(c), pD(c % 2), sc_xmb[c * 128:(c + 1) * 128, :])

        p4load(0)
        p4load(1)
        for c in range(NC):
            if half == 0:
                rb = rbv(c)
                wait("vector", lsem(c), p4ld[c])
                wait("vector", "lg" if c % 2 == 0 else "lg1", p4lg.get(c - 2, 0))
                if c >= 2:
                    wait("vector", "pe", p4pe[c - 2])
                do("vector", lambda E_, rb=rb: E_.tensor_sub(pA(2), rb, sM2), inc="dv")
                v = do("vector", lambda E_, c=c: E_.tensor_mul(pD(c % 2), pA(2), sR2), inc="dv")
                p4dv[c] = v
                wait("gpsimd", "dv", v)
                p4lg[c] = dma("gpsimd", "lg" if c % 2 == 0 else "lg1",
                              sc_xmb[c * 128:(c + 1) * 128, :], pD(c % 2))
                wait("tensor", "dv", v)
            else:
                wait("tensor", lsem(c), p4ld[c])
            src = pD(c % 2)[:, half * TH:(half + 1) * TH]
            pp = None
            for q in range(2):
                do("tensor", lambda E_, c=c, q=q, src=src: E_.matmul(
                    out=psA[:, q * 512:(q + 1) * 512], lhsT=xpT[:, c * 160:c * 160 + 128],
                    rhs=src[:, q * 512:(q + 1) * 512], start=(c == 0), stop=(c == NC - 1)), inc="pe")
                do("tensor", lambda E_, c=c, q=q, src=src: E_.matmul(
                    out=psB[0:16, q * 512:(q + 1) * 512], lhsT=xpT[:, c * 160 + 128:c * 160 + 144],
                    rhs=src[:, q * 512:(q + 1) * 512], start=(c == 0), stop=(c == NC - 1)), inc="pe")
                pp = do("tensor", lambda E_, c=c, q=q, src=src: E_.matmul(
                    out=psA[0:16, 1024 + q * 512:1024 + (q + 1) * 512],
                    lhsT=xpT[:, c * 160 + 144:c * 160 + 160],
                    rhs=src[:, q * 512:(q + 1) * 512], start=(c == 0), stop=(c == NC - 1)), inc="pe")
            p4pe[c] = pp
            p4load(c + 2)
        pxp = C["pe"]
        wait("vector", "pe", pxp)
        do("vector", lambda E_, half=half: E_.tensor_copy(dtin[:, half * TH:(half + 1) * TH],
                                                          psA[:, 0:1024]), inc="dv")
        do("vector", lambda E_, half=half: E_.tensor_copy(Bsb[:, half * TH:(half + 1) * TH],
                                                          psB[0:16, 0:1024]), inc="dv")
        do("vector", lambda E_, half=half: E_.tensor_copy(Csb[:, half * TH:(half + 1) * TH],
                                                          psA[0:16, 1024:2048]), inc="dv")
        barrier()

    for c in (range(NC) if max_phase >= 8 else ()):
        wait("sync", "dv", C["dv"])
        l = dma("sync", "lc", tmpc[:, 0:128], dt_proj_w[c * 128:(c + 1) * 128, :])
        wait("vector", "lc", l)
        wait("vector", "pe", C["pe"])
        v = do("vector", lambda E_: E_.tensor_copy(pD(0)[:, 0:128], tmpc[:, 0:128]), inc="dv")
        wait("tensor", "dv", v)
        p = do("tensor", lambda E_: E_.transpose(out=psDb[:, 0:128], in_=pD(0)[:, 0:128],
                                                 identity=identb), inc="pe")
        wait("vector", "pe", p)
        wait("vector", "lg", C["lg"])
        v = do("vector", lambda E_, c=c: E_.tensor_copy(pD(1)[:, 0:128], psDb[:, 0:128]), inc="dv")
        wait("gpsimd", "dv", v)
        dma("gpsimd", "lg", sc_dtp[c], pD(1)[:, 0:128])
    barrier()

    bst = poolE[:, 4096:6144]
    for n in (range(NS) if max_phase >= 9 else ()):
        for r in range(2):
            wait("tensor", "ac", C["ac"])
            wait("tensor", "dv", C["dv"])
            p = None
            for q in range(2):
                p = do("tensor", lambda E_, n=n, q=q, r=r: E_.matmul(
                    out=psB[:, q * 512:(q + 1) * 512], lhsT=indic[:, bass.ts(n, 128)],
                    rhs=Bsb[:, r * 1024 + q * 512:r * 1024 + (q + 1) * 512],
                    start=True, stop=True), inc="pe")
            wait("scalar", "pe", p)
            do("scalar", lambda E_, n=n, r=r: E_.copy(
                out=bst[:, r * 1024:(r + 1) * 1024], in_=psB), inc="ac")
        wait("vector", "ac", C["ac"])
        v = None
        for tau in range(16):
            v = do("vector", lambda E_, n=n, tau=tau: E_.tensor_mul(
                dgB(n, tau), identb, bst[:, tau * 128:(tau + 1) * 128]), inc="dv")
        for r in range(2):
            wait("tensor", "ac", C["ac"])
            p = None
            for q in range(2):
                p = do("tensor", lambda E_, n=n, q=q, r=r: E_.matmul(
                    out=psB[:, q * 512:(q + 1) * 512], lhsT=indic[:, bass.ts(n, 128)],
                    rhs=Csb[:, r * 1024 + q * 512:r * 1024 + (q + 1) * 512],
                    start=True, stop=True), inc="pe")
            wait("scalar", "pe", p)
            do("scalar", lambda E_, n=n, r=r: E_.copy(
                out=Cb_n(n)[:, r * 1024:(r + 1) * 1024], in_=psB), inc="ac")
    barrier()

    if scan_taps:
        sct = {nm: nc.dram_tensor(f"sct_{nm}", [2, 128, T], F32, kind="ExternalOutput").ap()
               for nm in ("dt", "dA", "h")}
    dt_f = pA(0)
    dA_b = (pA(1), pA(2))
    psAb = psA.bitcast(BF16)
    dbx2 = lambda th: psALL[:, th * 1024:(th + 1) * 1024]
    dty = psALL[:, 2048:4096]
    dtxT = lambda p: poolE[:, (4096 + p * 2048):(6144 + p * 2048)]
    dvhh = {}
    pyn = {}
    hc_po = {}
    stt_dv = {}
    se_cnt = {}
    pr2_po = {}
    lrs_cnt = {}
    yinit_ac = {}
    yinit_pe = {}
    pdt_cnt = {}
    pend = None

    def flush_y(pn, pnn, psid):
        wait("tensor", pn[0], pn[1])
        pyy = None
        for q in range(4):
            pyy = do("tensor", lambda E_, q=q, n=pnn: E_.matmul(
                out=dty[:, q * 512:(q + 1) * 512], lhsT=identb,
                rhs=cbuf(0)[:, q * 512:(q + 1) * 512],
                start=False, stop=(n == NS - 1)), inc="pe")
        pyn[psid] = pyy

    def emit_tail(cm):
        wait("vector", "lrs", lrs_cnt[cm])
        wait("vector", "pe", pyn[cm * NS + NS - 1])
        vr2 = do("vector", lambda E_: E_.scalar_tensor_tensor(
            pC(0), dty, 1.0, pC(0), op0=ALU.mult, op1=ALU.add), inc="dv")
        stt_dv[cm] = vr2
        wait("scalar", "dv", vr2)
        se_cnt[cm] = dma("scalar", "se", sc_r2[cm * 128:(cm + 1) * 128, :], pC(0))
        wait("gpsimd", "dv", vr2)
        wait("gpsimd", "sb2", C["sb2"])
        pr2 = do("gpsimd", lambda E_: E_.tensor_copy(r2stg, pC(0)), inc="po")
        pr2_po[cm] = pr2
        wait("scalar", "po", pr2)
        dma("scalar", "sb2", sc_r2b[cm * 128:(cm + 1) * 128, :], r2stg)
        if cm + 1 < NC:
            wait("sync", "se", se_cnt[cm])
            wait("sync", "po", pr2_po[cm])
            lrs_cnt[cm + 1] = dma("sync", "lrs", pC(0),
                                  sc_r1[(cm + 1) * 128:(cm + 2) * 128, :])

    if max_phase >= 10:
        lrs_cnt[0] = dma("sync", "lrs", pC(0), sc_r1[0:128, :])
    for c in (range(NC) if max_phase >= 10 else ()):
        dsem = "ldt0" if c % 2 == 0 else "ldt1"
        wait("sync", "pe", yinit_pe.get(c - 1, C["pe"]))
        lxm = dma("sync", "lxm", pD(1), sc_xmb[c * 128:(c + 1) * 128, :])
        if c >= 2:
            wait("sync", "pe", pdt_cnt[c - 2])
        ldt = dma("sync", dsem, dtpbuf[:, (c % 2) * 128:(c % 2 + 1) * 128], sc_dtp[c])
        acdt_h = {}
        dvdtx_h = {}
        vev_h = {}
        for hh in range(2):
            if (c * NS - 1, hh) in dvhh:
                wait("tensor", "dv", dvhh[(c * NS - 1, hh)])
            else:
                wait("tensor", "ac", C["ac"])
                wait("tensor", "dv", C["dv"])
            if hh == 0:
                wait("tensor", dsem, ldt)
            pdt = None
            for q in range(2):
                qq = hh * 2 + q
                pdt = do("tensor", lambda E_, c=c, qq=qq: E_.matmul(
                    out=psALL[:, qq * 512:(qq + 1) * 512],
                    lhsT=dtpbuf[:, (c % 2) * 128:(c % 2 + 1) * 128],
                    rhs=dtin[:, qq * 512:(qq + 1) * 512], start=True, stop=True), inc="pe")
            if hh == 1:
                pdt_cnt[c] = pdt
            wait("scalar", "pe", pdt)
            do("scalar", lambda E_, c=c, hh=hh: E_.activation(
                out=dt_f[:, hh * 1024:(hh + 1) * 1024],
                in_=psALL[:, hh * 1024:(hh + 1) * 1024],
                func=AF.Exp, bias=dtb_col[:, c:c + 1]), inc="ac")
            acdt_h[hh] = do("scalar", lambda E_, hh=hh: E_.activation(
                out=dt_f[:, hh * 1024:(hh + 1) * 1024],
                in_=dt_f[:, hh * 1024:(hh + 1) * 1024],
                func=AF.Ln, bias=1.0), inc="ac")
            wait("vector", "ac", acdt_h[hh])
            if hh == 0:
                wait("vector", "lxm", lxm)
            dvdtx_h[hh] = do("vector", lambda E_, hh=hh: E_.tensor_mul(
                pD(0)[:, hh * 1024:(hh + 1) * 1024],
                dt_f[:, hh * 1024:(hh + 1) * 1024],
                pD(1)[:, hh * 1024:(hh + 1) * 1024]), inc="dv")
            wait("tensor", "dv", dvdtx_h[hh])
            ptr = None
            for tau in range(hh * 8, hh * 8 + 8):
                ptr = do("tensor", lambda E_, tau=tau: E_.transpose(
                    out=psAb[:, tau * 128:(tau + 1) * 128], in_=pD(0)[:, bass.ts(tau, 128)],
                    identity=identb), inc="pe")
            wait("vector", "pe", ptr)
            vv = None
            for tau in range(hh * 8, hh * 8 + 8):
                vv = do("vector", lambda E_, c=c, tau=tau: E_.tensor_copy(
                    dtxT(c % 2)[:, bass.ts(tau, 128)], psAb[:, tau * 128:(tau + 1) * 128]), inc="dv")
            vev_h[hh] = vv
        wait("scalar", "dv", dvdtx_h[1])
        acdxm = do("scalar", lambda E_, c=c: E_.activation(out=pD(1), in_=pD(1), func=AF.Identity,
                                                           scale=D_col[:, c:c + 1]), inc="ac")
        yinit_ac[c] = acdxm
        if scan_taps and c < 2:
            dma("scalar", "stp", sct["dt"][c], dt_f)
        for n in range(NS):
            par = n % 2
            sid = c * NS + n
            if (sid - 2, 1) in dvhh:
                wait("scalar", "dv", dvhh[(sid - 2, 1)])
            acda = do("scalar", lambda E_, c=c, n=n, par=par: E_.activation(
                out=dA_b[par], in_=dt_f, func=AF.Exp,
                scale=A_sc[:, c * NS + n:c * NS + n + 1]), inc="ac")
            for th in range(2):
                if (sid - 1, th) in dvhh:
                    wait("tensor", "dv", dvhh[(sid - 1, th)])
                if n == 0 and th == 0:
                    wait("tensor", "dv", vev_h[1])
                pb = None
                for tt8 in range(8):
                    tau = th * 8 + tt8
                    pb = do("tensor", lambda E_, c=c, n=n, th=th, tau=tau: E_.matmul(
                        out=dbx2(th)[:, (tau % 8) * 128:((tau % 8) + 1) * 128],
                        lhsT=dtxT(c % 2)[:, bass.ts(tau, 128)],
                        rhs=dgB(n, tau), start=True, stop=True), inc="pe")
                if th == 1:
                    if n == 1:
                        wait("tensor", "ac", yinit_ac[c])
                        if (c - 1) in stt_dv:
                            wait("tensor", "dv", stt_dv[c - 1])
                        py = None
                        for q in range(4):
                            py = do("tensor", lambda E_, q=q: E_.matmul(
                                out=dty[:, q * 512:(q + 1) * 512], lhsT=identb,
                                rhs=pD(1)[:, q * 512:(q + 1) * 512], start=True, stop=False), inc="pe")
                        yinit_pe[c] = py
                    if pend is not None:
                        flush_y(*pend)
                        pend = None
                if th == 0:
                    wait("vector", "ac", acda)
                    if (sid - 2) in hc_po and hc_po[sid - 2][0] == "po":
                        wait("vector", "po", hc_po[sid - 2][1])
                wait("vector", "pe", pb)
                ini = 0.0 if th == 0 else hbuf(par)[:, 1023:1024]
                dvhh[(sid, th)] = do("vector", lambda E_, par=par, th=th, ini=ini: E_.tensor_tensor_scan(
                    hbuf(par)[:, th * 1024:(th + 1) * 1024],
                    dA_b[par][:, th * 1024:(th + 1) * 1024],
                    dbx2(th), ini, op0=ALU.mult, op1=ALU.add), inc="dv")
            if n == 0 and c > 0:
                emit_tail(c - 1)
            hce = "gpsimd" if n % 2 == 0 else "vector"
            if sid - 1 in pyn:
                wait(hce, "pe", pyn[sid - 1])
            if hce == "gpsimd":
                wait("gpsimd", "dv", dvhh[(sid, 1)])
                hc_po[sid] = ("po", do("gpsimd", lambda E_, n=n, par=par: E_.tensor_mul(
                    cbuf(par), hbuf(par), Cb_n(n)), inc="po"))
            else:
                hc_po[sid] = ("dv", do("vector", lambda E_, n=n, par=par: E_.tensor_mul(
                    cbuf(par), hbuf(par), Cb_n(n)), inc="dv"))
            pend = (hc_po[sid], n, sid)
            if scan_taps and c < 2 and n == 0:
                wait("scalar", "ac", acda)
                dma("scalar", "stp", sct["dA"][c], dA_b[par])
                vx = do("vector", lambda E_, par=par: E_.tensor_copy(tmpc[:, 0:1024], hbuf(par)[:, 0:1024]), inc="dv")
                wait("scalar", "dv", vx)
                dma("scalar", "stp", sct["h"][c][:, 0:1024], tmpc[:, 0:1024])
                wait("vector", "stp", C["stp"])
    if max_phase >= 10:
        if pend is not None:
            flush_y(*pend)
            pend = None
        emit_tail(NC - 1)
    barrier()

    sM3, sR3 = pA(0), pA(1)
    if max_phase >= 11:
        emit_ln_stats_fm(sc_r2b, sM3, sR3)
    barrier()

    if max_phase >= 12:
        l1 = dma("gpsimd", "lg", BfT, sc_bft)
        l2 = dma("scalar", "se", AfT, sc_aft)
        l3 = dma("scalar", "se", WgTb, sc_wgt)
    barrier()
    for h in range(2):
        if max_phase >= 13:
            emit_ffn_half(h, 2, sc_r2, sc_r3, None, sM=sM3, sR=sR3)
        barrier()

    p8pe = {}
    p8st = {}
    p8ld = {}
    stgv = lambda tt: poolA[:, 0:4096] if tt % 2 == 0 else stg_big
    xtv = lambda tt: (big[:, 0:8192] if tt % 2 == 0 else big[:, 16384:24576]).bitcast(F32)
    lsem8 = lambda tt: "lx" if tt % 2 == 0 else "lxm"
    ssem8 = lambda tt: "so" if tt % 2 == 0 else "stp"

    def p8load(tt):
        if tt >= NT:
            return
        wait("sync", "pe", p8pe.get(tt - 2, C["pe"] if tt < 2 else 0))
        p8ld[tt] = dma("sync", lsem8(tt), stgv(tt).rearrange("p (c t) -> p c t", t=128),
                       sc_r3[:, tt * 128:(tt + 1) * 128].rearrange("(c p) t -> p c t", p=128))

    if max_phase >= 14:
        p8load(0)
        p8load(1)
    for tt in (range(NT) if max_phase >= 14 else ()):
        stg = stgv(tt)
        xt = xtv(tt)
        wait("tensor", lsem8(tt), p8ld[tt])
        for b4 in range(8):
            p = None
            for i in range(4):
                c = b4 * 4 + i
                dst = psC if i < 2 else psD
                sl = (i % 2) * 256
                p = do("tensor", lambda E_, c=c, dst=dst, sl=sl, stg=stg: E_.transpose(
                    out=dst[:, sl:sl + 128], in_=stg[:, bass.ts(c, 128)],
                    identity=identf), inc="pe")
            if b4 == 7:
                p8pe[tt] = p
            wait("vector", "pe", p)
            if b4 == 0:
                wait("vector", ssem8(tt), p8st.get(tt - 2, 0))
            vv = None
            for i in range(4):
                c = b4 * 4 + i
                src = psC if i < 2 else psD
                sl = (i % 2) * 256
                vv = do("vector", lambda E_, c=c, src=src, sl=sl, xt=xt: E_.tensor_copy(
                    xt[:, bass.ts(c, 128)], src[:, sl:sl + 128]), inc="dv")
            wait("tensor", "dv", vv)
        wait("scalar", "dv", vv)
        p8st[tt] = dma("scalar", ssem8(tt), out_d[tt * 128:(tt + 1) * 128, :], xt)
        p8load(tt + 2)

    tapmap = {"r0": (sc_r0, False), "r1": (sc_r1, False), "r2": (sc_r2, False),
              "xmb": (sc_xmb, True)}
    for name in debug_taps:
        srcap, isbf = tapmap[name]
        tap = nc.dram_tensor(f"tap_{name}", [D, T], F32, kind="ExternalOutput").ap()
        for c in range(NC):
            wait("sync", "stp", C["stp"])
            if isbf:
                lc = dma("sync", "lx", pD(0), srcap[c * 128:(c + 1) * 128, :])
                wait("vector", "lx", lc)
                v = do("vector", lambda E_: E_.tensor_copy(pA(0), pD(0)), inc="dv")
            else:
                lc = dma("sync", "lx", pD(0).bitcast(F32)[:, 0:1024],
                         srcap[c * 128:(c + 1) * 128, 0:1024])
                lc2 = dma("sync", "lxm", pD(1).bitcast(F32)[:, 0:1024],
                          srcap[c * 128:(c + 1) * 128, 1024:2048])
                wait("vector", "lx", lc)
                wait("vector", "lxm", lc2)
                do("vector", lambda E_: E_.tensor_copy(pA(0)[:, 0:1024],
                                                       pD(0).bitcast(F32)[:, 0:1024]), inc="dv")
                v = do("vector", lambda E_: E_.tensor_copy(pA(0)[:, 1024:2048],
                                                           pD(1).bitcast(F32)[:, 0:1024]), inc="dv")
            wait("scalar", "dv", v)
            dma("scalar", "stp", tap[c * 128:(c + 1) * 128, :], pA(0))

    wait("gpsimd", "so", C["so"])
    wait("gpsimd", "stp", C["stp"])

    with nc.Block() as block:
        @block.sync
        def _(sync):
            for f in prog["sync"]:
                f(sync)

        @block.tensor
        def _(tensor):
            for f in prog["tensor"]:
                f(tensor)

        @block.vector
        def _(vector):
            for f in prog["vector"]:
                f(vector)

        @block.scalar
        def _(scalar):
            for f in prog["scalar"]:
                f(scalar)

        @block.gpsimd
        def _(gpsimd):
            for f in prog["gpsimd"]:
                f(gpsimd)

    return nc


def _consts():
    ident = np.eye(128, dtype=np.float32)
    ident_b = ident.copy()
    ones_b = np.ones((128, 128), dtype=np.float32)
    indic = np.zeros((16, 16 * 128), dtype=np.float32)
    for n in range(16):
        indic[n, n * 128:(n + 1) * 128] = 1.0
    expand = np.zeros((16, 128), dtype=np.float32)
    for e in range(E):
        expand[e, e * RK:(e + 1) * RK] = 1.0
    return ident, ident_b, ones_b, indic, expand


def kernel(L0_attn_x, L0_attn_A_log, L0_attn_D, L0_attn_x_proj_w, L0_attn_dt_proj_w,
           L0_attn_dt_proj_b, L0_ffn_W_base, L0_ffn_b_base, L0_ffn_W_gate, L0_ffn_W_thr,
           L0_ffn_b_thr, L0_ffn_A_w, L0_ffn_B_w, _debug_taps=(), _max_phase=99,
           _n_cores=8, _nc_cache={}):
    ident, ident_b, ones_b, indic, expand = _consts()
    x = np.ascontiguousarray(np.asarray(L0_attn_x, dtype=np.float32))
    cf = lambda a: np.ascontiguousarray(np.asarray(a, dtype=np.float32))
    shared = {
        "A_log": cf(L0_attn_A_log), "Dskip": cf(L0_attn_D),
        "x_proj_w": cf(L0_attn_x_proj_w), "dt_proj_w": cf(L0_attn_dt_proj_w),
        "dt_proj_b": cf(L0_attn_dt_proj_b), "W_base": cf(L0_ffn_W_base),
        "b_base": cf(L0_ffn_b_base), "W_gate": cf(L0_ffn_W_gate),
        "W_thr": cf(L0_ffn_W_thr), "b_thr": cf(L0_ffn_b_thr),
        "A_w": cf(L0_ffn_A_w).reshape(E * RK, D),
        "B_w": cf(L0_ffn_B_w).reshape(E * D, RK),
        "ident_f": ident, "ident_b": ident_b, "ones_b": ones_b,
        "indic_b": indic, "expand_b": expand,
    }
    key = (tuple(_debug_taps), _max_phase)
    if key not in _nc_cache:
        _nc_cache[key] = build_program(debug_taps=_debug_taps, max_phase=_max_phase)
    nc = _nc_cache[key]
    in_maps = [dict(shared, x=np.ascontiguousarray(x[b])) for b in range(_n_cores)]
    res = run_bass_kernel_spmd(nc, in_maps, core_ids=list(range(_n_cores)))
    out = np.stack([np.asarray(r["out"], dtype=np.float32) for r in res.results], axis=0)
    if _n_cores < 8:
        out = np.concatenate([out] + [np.zeros_like(out[:1])] * (8 - _n_cores), axis=0)
    if _debug_taps:
        taps = [{k: np.asarray(r[f"tap_{k}"]) for k in _debug_taps} for r in res.results]
        return out, taps
    return out


# version 17
# speedup vs baseline: 1.1914x; 1.0023x over previous
import numpy as np
import ml_dtypes
import concourse.bass as bass
import concourse.mybir as mybir
from concourse.bass_utils import run_bass_kernel_spmd

F32 = mybir.dt.float32
BF16 = mybir.dt.bfloat16
AF = mybir.ActivationFunctionType
ALU = mybir.AluOpType
AX = mybir.AxisListType
BF = ml_dtypes.bfloat16

T = 2048
D = 4096
NC = D // 128
NT = T // 128
NS = 16
DTR = 128
E = 8
RK = 16
TH = T // 2
EPS = 1e-5
MAXTHR = 0.125
SCALING = 2.0

ENGS = ("sync", "tensor", "vector", "scalar", "gpsimd")
ESEM = {"sync": "sy", "tensor": "pe", "vector": "dv", "scalar": "ac", "gpsimd": "po"}
BSEM = {"sync": "bsy", "tensor": "bpe", "vector": "bdv", "scalar": "bac", "gpsimd": "bpo"}
DSEMS = ("lc", "lx", "lw", "lw1", "lr", "lxm", "lrs", "ldt0", "ldt1", "lg",
         "s0", "sw", "sf", "sf1", "sb_", "se", "sb2", "so", "stp",
         "bl0", "bl1", "bl2", "bl3", "bl4", "bl5", "bl6", "bl7", "lg1", "lwg")


def build_program(debug_taps=(), max_phase=99, scan_taps=False):
    nc = bass.Bass("TRN2", target_bir_lowering=False, debug=False)

    di = lambda n, s: nc.dram_tensor(n, s, F32, kind="ExternalInput").ap()
    dib = lambda n, s: nc.dram_tensor(n, s, BF16, kind="ExternalInput").ap()
    x_in = di("x", [T, D])
    A_log = di("A_log", [D, NS])
    Dskip = di("Dskip", [D])
    x_proj_w = di("x_proj_w", [DTR + 2 * NS, D])
    dt_proj_w = di("dt_proj_w", [D, DTR])
    dt_proj_b = di("dt_proj_b", [D])
    W_base = di("W_base", [D, D])
    b_base = di("b_base", [D])
    W_gate = di("W_gate", [E, D])
    W_thr = di("W_thr", [D])
    b_thr = di("b_thr", [1])
    A_w = di("A_w", [E * RK, D])
    B_w = di("B_w", [E * D, RK])
    ident_f = di("ident_f", [128, 128])
    ident_bd = di("ident_b", [128, 128])
    ones_bd = di("ones_b", [128, 128])
    indic_bd = di("indic_b", [16, 16 * 128])
    expand_bd = di("expand_b", [16, 128])
    out_d = nc.dram_tensor("out", [T, D], F32, kind="ExternalOutput").ap()

    ds = lambda n, s, dt=F32: nc.dram_tensor(n, s, dt, kind="Internal").ap()
    sc_r0 = ds("sc_r0", [D, T])
    sc_r1 = ds("sc_r1", [D, T])
    sc_r1b = ds("sc_r1b", [D, T], BF16)
    sc_r2 = ds("sc_r2", [D, T])
    sc_r2b = ds("sc_r2b", [D, T], BF16)
    sc_r3 = ds("sc_r3", [D, T])
    sc_xmb = ds("sc_xmb", [D, T], BF16)
    sc_wt = ds("sc_wt", [NC, 128, D], BF16)
    sc_dtp = ds("sc_dtp", [NC, 128, 128], BF16)
    sc_bft = ds("sc_bft", [128, D], BF16)
    sc_aft = ds("sc_aft", [128, D], BF16)
    sc_wgt = ds("sc_wgt", [128, 288], BF16)

    sb = lambda n, s, dt=F32: nc.alloc_sbuf_tensor(n, s, dt).ap()
    identf = sb("identf", [128, 128])
    identb = sb("identb", [128, 128], BF16)
    ones128 = sb("ones128", [128, 128], BF16)
    indic = sb("indic", [16, 16 * 128], BF16)
    expand = sb("expand", [16, 128], BF16)
    A_sc = sb("A_sc", [128, NC * NS])
    D_col = sb("D_col", [128, NC])
    dtb_col = sb("dtb_col", [128, NC])
    bb_col = sb("bb_col", [128, NC])
    thrb = sb("thrb", [128, 1])
    dtin = sb("dtin", [128, T], BF16)
    dtpbuf = sb("dtpbuf", [128, 256], BF16)
    colt = sb("colt", [128, 8])
    ceps = sb("ceps", [128, 1])
    tmpc = sb("tmpc", [128, 1024])

    big = sb("big", [128, 65536], BF16)
    xln = big[:, 0:NC * TH]
    wld_f = big[:, 32768:40960].bitcast(F32)
    wtt = [big[:, 40960:45056], big[:, 45056:49152]]
    ln_stg = big[:, 40960:45056]
    g_hw = big[:, 49152:50176]
    g_h = big[:, 50176:51200]
    AfT = big[:, 51200:55296]
    xpT = big[:, 55296:60416]
    WgTb = big[:, 60416:60704]
    wcast = big[:, 60704:64800]
    dgB = lambda n, tau: big[:, n * T + tau * 128:n * T + (tau + 1) * 128]
    Cb_n = lambda n: big[:, 32768 + n * T:32768 + (n + 1) * T]

    poolA = sb("poolA", [128, 6144])
    pA = lambda i: poolA[:, i * 2048:(i + 1) * 2048]
    poolC = sb("poolC", [128, 2048])
    pC = lambda i: poolC[:, i * 2048:(i + 1) * 2048]
    poolD = sb("poolD", [128, 4096], BF16)
    pD = lambda i: poolD[:, i * 2048:(i + 1) * 2048]
    poolE = sb("poolE", [128, 8192], BF16)
    BfT = poolE[:, 0:4096]
    hbuf = lambda p: poolE[:, p * 2048:(p + 1) * 2048]
    xbuf = lambda p: poolE[:, (2 + p) * 2048:(3 + p) * 2048]
    Bsb = poolE[0:16, 0:2048]
    Csb = poolE[0:16, 2048:4096]
    poolF = sb("poolF", [128, 2048], BF16)
    r2stg = tmpc.bitcast(BF16)
    cbuf = lambda p: poolF[:, 0:2048]
    stage = poolA[0:32, 0:4096]
    stage2 = poolF[0:16, 0:2048]
    stageB = poolE[0:16, 4096:6144]
    wldA = poolA[:, 0:4096]
    wcast2 = poolE[:, 4096:8192]

    psALL = nc.alloc_psum_tensor("psALL", [128, 4096], F32).ap()
    psA = psALL[:, 0:2048]
    psB = psALL[:, 2048:3072]
    psC = psALL[:, 3072:3584]
    psD = psALL[:, 3584:4096]
    psDb = psD.bitcast(BF16)
    psCb = psC.bitcast(BF16)
    dbxps = lambda p: psALL[:, p * 2048:(p + 1) * 2048]

    S = {k: nc.alloc_semaphore(k) for k in
         ("pe", "dv", "ac", "po", "sy", "bpe", "bdv", "bac", "bpo", "bsy") + DSEMS}
    C = {k: 0 for k in S}
    prog = {e: [] for e in ENGS}

    def on(eng, f):
        prog[eng].append(f)

    def wait(eng, sem, val):
        if val > 0:
            on(eng, lambda E_, s=S[sem], v=val: E_.wait_ge(s, v))

    def do(eng, f, inc=None, n=1):
        es = ESEM[eng]
        prev = C[es]
        selfwait = (eng != "tensor")
        if inc is None:
            if selfwait and prev > 0:
                on(eng, lambda E_, f=f, s=S[es], p=prev: f(E_)._wait_ge(s, p))
            else:
                on(eng, f)
            return None
        C[inc] += n
        tgt = C[inc]
        if selfwait and prev > 0:
            on(eng, lambda E_, f=f, s=S[es], p=prev, si=S[inc], m=n:
               f(E_)._wait_ge(s, p).then_inc(si, m))
        else:
            on(eng, lambda E_, f=f, si=S[inc], m=n: f(E_).then_inc(si, m))
        return tgt

    def dma(eng, sem, out, in_):
        prev = C[sem]
        C[sem] += 16
        tgt = C[sem]
        on(eng, lambda E_, o=out, i=in_, s=S[sem], p=prev:
           ((E_.dma_start(out=o, in_=i)._wait_ge(s, p) if p > 0
             else E_.dma_start(out=o, in_=i)).then_inc(s, 16)))
        return tgt

    def barrier():
        snap = {e: C[BSEM[e]] + 1 for e in ENGS}
        dsnap = {s: C[s] for s in DSEMS}
        for e in ENGS:
            do(e, lambda E_: E_.nop(), inc=BSEM[e])
            for o in ENGS:
                if o != e:
                    wait(e, BSEM[o], snap[o])
            for s in DSEMS:
                wait(e, s, dsnap[s])

    nc.const_aps.register(F32, EPS, ceps)

    li = dma("sync", "lc", identf, ident_f)
    do("vector", lambda E_: E_.memset(ceps, EPS), inc="dv")
    lc = dma("sync", "lc", tmpc[:, 0:128], ident_bd)
    wait("vector", "lc", lc)
    do("vector", lambda E_: E_.tensor_copy(identb, tmpc[:, 0:128]), inc="dv")
    lc = dma("sync", "lc", stage[0:16, 0:2048], indic_bd)
    wait("vector", "lc", lc)
    do("vector", lambda E_: E_.tensor_copy(indic, stage[0:16, 0:2048]), inc="dv")
    wait("sync", "dv", C["dv"])
    lc = dma("sync", "lc", tmpc[:, 0:128], ones_bd)
    wait("vector", "lc", lc)
    do("vector", lambda E_: E_.tensor_copy(ones128, tmpc[:, 0:128]), inc="dv")
    lc = dma("sync", "lc", stage[0:16, 0:128], expand_bd)
    wait("vector", "lc", lc)
    do("vector", lambda E_: E_.tensor_copy(expand, stage[0:16, 0:128]), inc="dv")
    wait("sync", "dv", C["dv"])
    ldA = dma("sync", "lc", A_sc.rearrange("p (c n) -> p c n", c=NC),
              A_log.rearrange("(c p) n -> p c n", p=128))
    wait("scalar", "lc", ldA)
    acA = do("scalar", lambda E_: E_.activation(out=A_sc, in_=A_sc, func=AF.Exp), inc="ac")
    wait("vector", "ac", acA)
    do("vector", lambda E_: E_.tensor_scalar_mul(A_sc, A_sc, -1.0), inc="dv")

    def load_cols(dst, src, scale=None):
        l = dma("sync", "lc", stage[0:32, 0:128], src.rearrange("(c p) -> c p", p=128))
        wait("tensor", "lc", l)
        wait("tensor", "dv", C["dv"])
        p = do("tensor", lambda E_: E_.transpose(out=psC[:, 0:32], in_=stage[0:32, 0:128],
                                                 identity=identf[0:32, 0:32]), inc="pe")
        wait("vector", "pe", p)
        if scale is None:
            do("vector", lambda E_, d=dst: E_.tensor_copy(d, psC[:, 0:32]), inc="dv")
        else:
            do("vector", lambda E_, d=dst, s=scale: E_.tensor_scalar_mul(d, psC[:, 0:32], s), inc="dv")
        wait("sync", "dv", C["dv"])

    if max_phase >= -2:
        load_cols(D_col, Dskip)
        load_cols(dtb_col, dt_proj_b)
        load_cols(bb_col, b_base, scale=0.5)

    if max_phase >= -2:
        v = do("vector", lambda E_: E_.memset(pD(0)[:, 0:1], 0.0), inc="dv")
        wait("gpsimd", "dv", C["dv"])
        l = dma("gpsimd", "lg", stage[0:1, 0:1], b_thr.rearrange("(a x) -> a x", a=1))
        wait("vector", "lg", l)
        v = do("vector", lambda E_: E_.tensor_copy(pD(0)[0:1, 0:1], stage[0:1, 0:1]), inc="dv")
        wait("tensor", "dv", v)
        p = do("tensor", lambda E_: E_.matmul(out=psC[:, 32:33], lhsT=ones128, rhs=pD(0)[:, 0:1],
                                              start=True, stop=True), inc="pe")
        wait("vector", "pe", p)
        do("vector", lambda E_: E_.tensor_copy(thrb, psC[:, 32:33]), inc="dv")

    stg_big = big[:, 8192:16384].bitcast(F32)

    def rows_to_bf(dst_bf_rows, src_dram, nrows, stg=None):
        stg = stg if stg is not None else stage
        ng = (nrows + 31) // 32
        for g in range(ng):
            r0, r1 = 32 * g, min(32 * (g + 1), nrows)
            l = dma("sync", "lc", stg[0:r1 - r0, 0:4096], src_dram[r0:r1, :])
            wait("vector", "lc", l)
            do("vector", lambda E_, g=g, r0=r0, r1=r1, dst=dst_bf_rows, stg=stg:
               E_.tensor_copy(dst(r0, r1), stg[0:r1 - r0, 0:4096]), inc="dv")
            wait("sync", "dv", C["dv"])

    def transpose_rows(src_rows_bf, ncols_rows, dst_fn):
        wait("tensor", "dv", C["dv"])
        idr = identb[0:ncols_rows, 0:ncols_rows]
        for c in range(NC):
            p = do("tensor", lambda E_, c=c, idr=idr: E_.transpose(
                out=psDb[:, 0:ncols_rows], in_=src_rows_bf(c), identity=idr), inc="pe")
            wait("vector", "pe", p)
            v = do("vector", lambda E_, c=c: E_.tensor_copy(dst_fn(c), psDb[:, 0:ncols_rows]), inc="dv")
            wait("tensor", "dv", v)

    def emit_small_weights():
        wait("sync", "dv", C["dv"])
        wait("sync", "pe", C["pe"])
        l = dma("sync", "lc", stg_big[0:8, 0:4096], W_gate)
        l = dma("sync", "lc", stg_big[8:9, 0:4096], W_thr.rearrange("(a d) -> a d", a=1))
        wait("vector", "lc", l)
        v = do("vector", lambda E_: E_.tensor_copy(big[0:9, 0:4096], stg_big[0:9, 0:4096]), inc="dv")
        transpose_rows(lambda c: big[0:9, bass.ts(c, 128)], 9,
                       lambda c: WgTb[:, c * 9:(c + 1) * 9])
        wait("sync", "dv", C["dv"])
        rows_to_bf(lambda r0, r1: big[r0:r1, 0:4096], A_w, 128, stg=stg_big)
        transpose_rows(lambda c: big[0:128, bass.ts(c, 128)], 128,
                       lambda c: AfT[:, bass.ts(c, 128)])

    if max_phase >= -1:
        emit_small_weights()

    if max_phase >= -1:
        rows_to_bf(lambda r0, r1: big[r0:r1, 0:4096], x_proj_w, 128)
    for g in (range(1) if max_phase >= -1 else ()):
        l = dma("sync", "lc", stage[0:32, 0:4096], x_proj_w[128:160, :])
        wait("vector", "lc", l)
        do("vector", lambda E_: E_.tensor_copy(big[0:32, 4096:8192], stage[0:32, 0:4096]), inc="dv")
        wait("sync", "dv", C["dv"])
    if max_phase >= -1:
        transpose_rows(lambda c: big[0:128, bass.ts(c, 128)], 128,
                       lambda c: xpT[:, c * 160:c * 160 + 128])
        wait("tensor", "dv", C["dv"])
        for c in range(NC):
            p = do("tensor", lambda E_, c=c: E_.transpose(
                out=psDb[:, 0:32], in_=big[0:32, 4096 + 128 * c:4096 + 128 * (c + 1)],
                identity=identb[0:32, 0:32]), inc="pe")
            wait("vector", "pe", p)
            v = do("vector", lambda E_, c=c: E_.tensor_copy(
                xpT[:, c * 160 + 128:(c + 1) * 160], psDb[:, 0:32]), inc="dv")
            wait("tensor", "dv", v)


    def emit_bft(upto=99):
        engs8 = ["sync", "gpsimd", "scalar"]
        sems8 = ["bl0", "bl1", "bl2", "bl3", "bl4", "bl5", "bl6", "bl7"]
        lds = []
        for e in range(E):
            eng = engs8[e % 3]
            wait(eng, "dv", C["dv"])
            wait(eng, "pe", C["pe"])
            l = None
            for c in range(NC):
                l = dma(eng, sems8[e],
                        stg_big[:, e * 512 + c * RK:e * 512 + (c + 1) * RK],
                        B_w[(e * D + c * 128):(e * D + (c + 1) * 128), :])
            lds.append(l)
        slots = [psC[0:16, 0:128], psC[0:16, 128:256], psD[0:16, 0:128], psD[0:16, 128:256]]
        bufs = [(stage2, "lg"), (stageB, "lg1")]
        pe_i = {}
        dv_i = {}
        st_dma = {0: 0, 1: 0}
        if upto < 1:
            return
        elim = upto if 2 < upto < 90 else (upto - 100 if upto >= 100 else E)
        wait("tensor", "dv", C["dv"])
        for e in range(min(E, elim)):
            wait("tensor", sems8[e], lds[e])
            for hf in range(2):
                bi = (e * 2 + hf) % 2
                buf, bsem = bufs[bi]
                wait("vector", bsem, st_dma[bi])
                for i in range(16):
                    g = e * 32 + hf * 16 + i
                    if (g - 4) in dv_i:
                        wait("tensor", "dv", dv_i[g - 4])
                    src = stg_big[:, e * 512 + hf * 256 + i * 16:e * 512 + hf * 256 + (i + 1) * 16]
                    pe_i[g] = do("tensor", lambda E_, src=src, sl=slots[i % 4]: E_.transpose(
                        out=sl, in_=src, identity=identf), inc="pe")
                    wait("vector", "pe", pe_i[g])
                    dv_i[g] = do("vector", lambda E_, i=i, sl=slots[i % 4], buf=buf: E_.tensor_scalar_mul(
                        buf[0:16, i * 128:(i + 1) * 128], sl, SCALING), inc="dv")
                if upto < 2:
                    continue
                wait("gpsimd", "dv", dv_i[e * 32 + hf * 16 + 15])
                st_dma[bi] = dma("gpsimd", bsem,
                                 BfT[e * 16:(e + 1) * 16, hf * 2048:(hf + 1) * 2048],
                                 buf[0:16, 0:2048])

    if max_phase >= 1:
        emit_bft()
        wait("scalar", "dv", C["dv"])
        dma("scalar", "se", sc_aft, AfT)
        dma("scalar", "se", sc_wgt, WgTb)
        wait("gpsimd", "lg", C["lg"])
        wait("gpsimd", "lg1", C["lg1"])
        dma("gpsimd", "lg", sc_bft, BfT)
    barrier()

    def ln_stats_tm(xt):
        do("vector", lambda E_, xt=xt: E_.reduce_sum(colt[:, 2:3], xt, axis=AX.X), inc="dv")
        do("scalar", lambda E_, xt=xt: E_.activation(out=pD(0), in_=xt[:, 0:2048],
                                                     func=AF.Square, accum_out=colt[:, 3:4]), inc="ac")
        a = do("scalar", lambda E_, xt=xt: E_.activation(out=pD(1), in_=xt[:, 2048:4096],
                                                         func=AF.Square, accum_out=colt[:, 4:5]), inc="ac")
        wait("vector", "ac", a)
        do("vector", lambda E_: E_.tensor_scalar_mul(colt[:, 2:3], colt[:, 2:3], 1.0 / D), inc="dv")
        do("vector", lambda E_: E_.tensor_add(colt[:, 3:4], colt[:, 3:4], colt[:, 4:5]), inc="dv")
        do("vector", lambda E_: E_.tensor_scalar_mul(colt[:, 3:4], colt[:, 3:4], 1.0 / D), inc="dv")
        do("vector", lambda E_: E_.tensor_mul(colt[:, 4:5], colt[:, 2:3], colt[:, 2:3]), inc="dv")
        v = do("vector", lambda E_: E_.tensor_sub(colt[:, 3:4], colt[:, 3:4], colt[:, 4:5]), inc="dv")
        wait("scalar", "dv", v)
        do("scalar", lambda E_: E_.activation(out=colt[:, 3:4], in_=colt[:, 3:4], func=AF.Ln, bias=EPS), inc="ac")
        a = do("scalar", lambda E_: E_.activation(out=colt[:, 0:1], in_=colt[:, 3:4], func=AF.Exp, scale=-0.5), inc="ac")
        wait("vector", "ac", a)
        do("vector", lambda E_: E_.tensor_mul(colt[:, 1:2], colt[:, 2:3], colt[:, 0:1]), inc="dv")
        return do("vector", lambda E_: E_.tensor_scalar_mul(colt[:, 1:2], colt[:, 1:2], -1.0), inc="dv")

    def emit_ln1_half(h):
        xtb = lambda k: wldA if k % 2 == 0 else wld_f
        tdone = {}
        ld = {}
        s0_cnt = {}

        def ln1_load(k):
            if k >= 8:
                return
            eng, sem = ("sync", "lx") if k % 2 == 0 else ("gpsimd", "lg")
            pe_, dv_, ac_ = tdone.get(k - 2, (C["pe"], C["dv"], C["ac"]))
            wait(eng, "pe", pe_)
            wait(eng, "dv", dv_)
            wait(eng, "ac", ac_)
            ld[k] = dma(eng, sem, xtb(k),
                        x_in[(h * 8 + k) * 128:(h * 8 + k + 1) * 128, :])

        ln1_load(0)
        ln1_load(1)
        for k in range(8):
            tt = h * 8 + k
            xt = xtb(k)
            ksem = "lx" if k % 2 == 0 else "lg"
            wait("vector", ksem, ld[k])
            wait("scalar", ksem, ld[k])
            dvst = ln_stats_tm(xt)
            wait("vector", "ac", C["ac"])
            wait("vector", "pe", C["pe"])
            va = do("vector", lambda E_, xt=xt: E_.tensor_scalar(
                ln_stg, xt, colt[:, 0:1], colt[:, 1:2], op0=ALU.mult, op1=ALU.add), inc="dv")
            wait("tensor", "dv", va)
            toff = k * 128
            for b8 in range(4):
                p = None
                for i in range(8):
                    c = b8 * 8 + i
                    dst = psDb if i < 4 else psCb
                    sl = (i % 4) * 256
                    p = do("tensor", lambda E_, c=c, dst=dst, sl=sl: E_.transpose(
                        out=dst[:, sl:sl + 128], in_=ln_stg[:, bass.ts(c, 128)],
                        identity=identb), inc="pe")
                wait("vector", "pe", p)
                vv = None
                for i in range(8):
                    c = b8 * 8 + i
                    src = psDb if i < 4 else psCb
                    sl = (i % 4) * 256
                    vv = do("vector", lambda E_, c=c, src=src, sl=sl, toff=toff: E_.tensor_copy(
                        xln[:, c * TH + toff:c * TH + toff + 128], src[:, sl:sl + 128]), inc="dv")
                wait("tensor", "dv", vv)
            for b8 in range(8):
                g = k * 8 + b8
                p = None
                for i in range(4):
                    c = b8 * 4 + i
                    dst = psC if i < 2 else psD
                    sl = (i % 2) * 256
                    p = do("tensor", lambda E_, c=c, xt=xt, dst=dst, sl=sl: E_.transpose(
                        out=dst[:, sl:sl + 128], in_=xt[:, bass.ts(c, 128)],
                        identity=identf), inc="pe")
                wait("vector", "pe", p)
                if (g - 2) in s0_cnt:
                    wait("vector", "s0", s0_cnt[g - 2])
                vv = None
                to = (b8 % 2) * 512
                for i in range(4):
                    dst = psC if i < 2 else psD
                    sl = (i % 2) * 256
                    vv = do("vector", lambda E_, i=i, dst=dst, sl=sl, to=to: E_.tensor_copy(
                        tmpc[:, to + i * 128:to + (i + 1) * 128], dst[:, sl:sl + 128]), inc="dv")
                wait("tensor", "dv", vv)
                wait("scalar", "dv", vv)
                s0_cnt[g] = dma("scalar", "s0",
                                sc_r0[b8 * 4 * 128:(b8 * 4 + 4) * 128, tt * 128:(tt + 1) * 128]
                                .rearrange("(c p) t -> p c t", p=128),
                                tmpc[:, to:to + 512].rearrange("p (c t) -> p c t", t=128))
            ln1_load(k + 2)
            tdone[k] = (C["pe"], C["dv"], C["ac"])

    def emit_gate_path(src_bf):
        wait("tensor", "dv", C["dv"])
        wait("tensor", "ac", C["ac"])
        plg = None
        for c in range(NC):
            for q in range(2):
                plg = do("tensor", lambda E_, c=c, q=q: E_.matmul(
                    out=psB[0:9, q * 512:(q + 1) * 512], lhsT=WgTb[:, c * 9:(c + 1) * 9],
                    rhs=src_bf[:, c * TH + q * 512:c * TH + (q + 1) * 512],
                    start=(c == 0), stop=(c == NC - 1)), inc="pe")
        ph = None
        for c in range(NC):
            for q in range(2):
                ph = do("tensor", lambda E_, c=c, q=q: E_.matmul(
                    out=psA[:, q * 512:(q + 1) * 512], lhsT=AfT[:, bass.ts(c, 128)],
                    rhs=src_bf[:, c * TH + q * 512:c * TH + (q + 1) * 512],
                    start=(c == 0), stop=(c == NC - 1)), inc="pe")
        wait("scalar", "pe", ph)
        ach = do("scalar", lambda E_: E_.activation(out=g_h, in_=psA[:, 0:1024], func=AF.Identity), inc="ac")
        wait("vector", "pe", plg)
        vl = do("vector", lambda E_: E_.tensor_copy(tmpc[0:9, 0:1024], psB[0:9, 0:1024]), inc="dv")
        wait("tensor", "dv", vl)
        for q in range(8):
            p = do("tensor", lambda E_, q=q: E_.transpose(
                out=psC[:, 0:9], in_=tmpc[0:9, q * 128:(q + 1) * 128],
                identity=identf[0:9, 0:9]), inc="pe")
            wait("vector", "pe", p)
            v = do("vector", lambda E_, q=q: E_.tensor_copy(poolC[:, q * 16:q * 16 + 9], psC[:, 0:9]), inc="dv")
            wait("tensor", "dv", v)
        for q in range(8):
            lg = poolC[:, q * 16:q * 16 + 8]
            th = poolC[:, q * 16 + 8:q * 16 + 9]
            do("vector", lambda E_, lg=lg: E_.reduce_max(colt[:, 5:6], lg, axis=AX.X), inc="dv")
            v = do("vector", lambda E_: E_.tensor_scalar_mul(colt[:, 5:6], colt[:, 5:6], -1.0), inc="dv")
            wait("scalar", "dv", v)
            do("scalar", lambda E_, lg=lg: E_.activation(out=lg, in_=lg, func=AF.Exp, bias=colt[:, 5:6]), inc="ac")
            a = do("scalar", lambda E_, th=th: E_.activation(out=th, in_=th, func=AF.Sigmoid, bias=thrb), inc="ac")
            wait("vector", "ac", a)
            do("vector", lambda E_, lg=lg: E_.reduce_sum(colt[:, 5:6], lg, axis=AX.X), inc="dv")
            do("vector", lambda E_: E_.reciprocal(colt[:, 6:7], colt[:, 5:6]), inc="dv")
            do("vector", lambda E_, th=th: E_.tensor_scalar_mul(th, th, MAXTHR), inc="dv")
            do("vector", lambda E_, lg=lg: E_.tensor_scalar_mul(lg, lg, colt[:, 6:7]), inc="dv")
            do("vector", lambda E_, lg=lg, th=th: E_.tensor_scalar(
                lg, lg, th, None, op0=ALU.subtract), inc="dv")
            do("vector", lambda E_, lg=lg: E_.tensor_scalar_max(lg, lg, 0.0), inc="dv")
            do("vector", lambda E_, lg=lg: E_.reduce_sum(colt[:, 5:6], lg, axis=AX.X), inc="dv")
            do("vector", lambda E_: E_.tensor_single_scalar(colt[:, 6:7], colt[:, 5:6], 0.0, op=ALU.is_equal), inc="dv")
            do("vector", lambda E_: E_.tensor_add(colt[:, 5:6], colt[:, 5:6], colt[:, 6:7]), inc="dv")
            do("vector", lambda E_: E_.reciprocal(colt[:, 6:7], colt[:, 5:6]), inc="dv")
            do("vector", lambda E_, lg=lg: E_.tensor_scalar_mul(lg, lg, colt[:, 6:7]), inc="dv")
        v = do("vector", lambda E_: E_.memset(stage2[0:16, 0:1024], 0.0), inc="dv")
        for q in range(8):
            v = do("vector", lambda E_, q=q: E_.tensor_copy(pD(1)[:, q * 16:q * 16 + 8],
                                                            poolC[:, q * 16:q * 16 + 8]), inc="dv")
        wait("tensor", "dv", v)
        for q in range(8):
            p = do("tensor", lambda E_, q=q: E_.transpose(
                out=psDb[0:8, 0:128], in_=pD(1)[:, q * 16:q * 16 + 8], identity=identb), inc="pe")
            wait("vector", "pe", p)
            v = do("vector", lambda E_, q=q: E_.tensor_copy(stage2[0:8, q * 128:(q + 1) * 128],
                                                            psDb[0:8, 0:128]), inc="dv")
            wait("tensor", "dv", v)
        pw = None
        for q in range(2):
            pw = do("tensor", lambda E_, q=q: E_.matmul(
                out=psB[:, q * 512:(q + 1) * 512], lhsT=expand,
                rhs=stage2[0:16, q * 512:(q + 1) * 512], start=True, stop=True), inc="pe")
        wait("vector", "pe", pw)
        wait("vector", "ac", ach)
        return (do("vector", lambda E_: E_.tensor_mul(g_hw, g_h, psB[:, 0:1024]), inc="dv"), ach)

    def emit_ffn_half(h, phase, resid_in, resid_out, resid_out_bf, sM=None, sR=None):
        if sM is not None:
            xb_dv = {}
            xld = {}
            xsem = lambda c: "lr" if c % 2 == 0 else "lrs"

            def xload(c):
                if c >= NC:
                    return
                if c >= 2:
                    wait("sync", "dv", xb_dv[c - 2])
                else:
                    wait("sync", "dv", C["dv"])
                xld[c] = dma("sync", xsem(c), poolC[:, (c % 2) * 1024:(c % 2 + 1) * 1024],
                             resid_in[c * 128:(c + 1) * 128, h * TH:(h + 1) * TH])

            xload(0)
            xload(1)
            for c in range(NC):
                rb = poolC[:, (c % 2) * 1024:(c % 2 + 1) * 1024]
                wait("vector", xsem(c), xld[c])
                do("vector", lambda E_, rb=rb, h=h: E_.tensor_sub(
                    tmpc[:, 0:1024], rb, sM[:, h * TH:(h + 1) * TH]), inc="dv")
                xb_dv[c] = do("vector", lambda E_, c=c, h=h: E_.tensor_mul(
                    xln[:, c * TH:(c + 1) * TH], tmpc[:, 0:1024], sR[:, h * TH:(h + 1) * TH]), inc="dv")
                xload(c + 2)
        chain = (phase == 1 and h == 0)
        wldb = lambda j: wld_f if j % 2 == 0 else wldA
        wcb = lambda j: wcast if j % 2 == 0 else wcast2
        wlsem = lambda j: "lw" if j % 2 == 0 else "lwg"
        wleng = lambda j: "sync" if j % 2 == 0 else "gpsimd"
        swsem = lambda j: "sw" if j % 2 == 0 else "s0"
        cast_ac = {}
        tr_pe = {}
        st_cnt = {}
        wch_ld = {}

        def wch_load(j):
            if j >= NC:
                return
            eng = wleng(j)
            if j >= 2:
                wait(eng, "ac", cast_ac[j - 2])
            else:
                wait(eng, "pe", C["pe"])
                wait(eng, "dv", C["dv"])
            wch_ld[j] = dma(eng, wlsem(j), wldb(j), W_base[j * 128:(j + 1) * 128, :])

        def wch_cast(j):
            if j >= NC:
                return
            wait("scalar", wlsem(j), wch_ld[j])
            if j >= 2:
                wait("scalar", "pe", tr_pe[j - 2])
            cast_ac[j] = do("scalar", lambda E_, j=j: E_.activation(
                out=wcb(j), in_=wldb(j), func=AF.Identity), inc="ac")

        if chain:
            wch_load(0)
            wch_load(1)
        hwv, ach_h = emit_gate_path(xln)
        if chain:
            wch_cast(0)
            wch_cast(1)
        evac_dv = {}
        psAh = lambda p: psA[:, p * 1024:(p + 1) * 1024]
        lwsem = lambda j: "lw" if j % 2 == 0 else "lw1"
        ldw = {}
        mm_done = {}
        evac_ac = {}

        def issue_wload(j):
            if j >= NC or chain:
                return
            if j >= 2:
                wait("sync", "pe", mm_done[j - 2])
            ldw[j] = dma("sync", lwsem(j), wtt[j % 2], sc_wt[j])

        issue_wload(0)
        issue_wload(1)
        for j in range(NC):
            wtbuf = wtt[j % 2]
            if chain:
                wait("tensor", "ac", cast_ac[j])
                for b8 in range(4):
                    p = None
                    for i in range(8):
                        c = b8 * 8 + i
                        dst = psDb if i < 4 else psCb
                        sl = (i % 4) * 256
                        p = do("tensor", lambda E_, c=c, dst=dst, sl=sl, j=j: E_.transpose(
                            out=dst[:, sl:sl + 128], in_=wcb(j)[:, bass.ts(c, 128)],
                            identity=identb), inc="pe")
                    if b8 == 3:
                        tr_pe[j] = p
                    wait("vector", "pe", p)
                    if b8 == 0 and j >= 2:
                        wait("vector", swsem(j), st_cnt[j - 2])
                    vv = None
                    for i in range(8):
                        c = b8 * 8 + i
                        srcp = psDb if i < 4 else psCb
                        sl = (i % 4) * 256
                        vv = do("vector", lambda E_, c=c, srcp=srcp, sl=sl, wtbuf=wtbuf: E_.tensor_copy(
                            wtbuf[:, bass.ts(c, 128)], srcp[:, sl:sl + 128]), inc="dv")
                    wait("tensor", "dv", vv)
                wait("scalar", "dv", vv)
                st_cnt[j] = dma("scalar", swsem(j), sc_wt[j], wtbuf)
            else:
                wait("tensor", lwsem(j), ldw[j])
            if j == 0:
                wait("tensor", "ac", ach_h)
            if j >= 2:
                wait("tensor", "ac", evac_ac[j - 2])
            for c in range(NC):
                do("tensor", lambda E_, c=c, j=j, wtbuf=wtbuf: E_.matmul(
                    out=psAh(j % 2)[:, 0:512], lhsT=wtbuf[:, bass.ts(c, 128)],
                    rhs=xln[:, c * TH:c * TH + 512], start=(c == 0), stop=False), inc="pe")
                do("tensor", lambda E_, c=c, j=j, wtbuf=wtbuf: E_.matmul(
                    out=psAh(j % 2)[:, 512:1024], lhsT=wtbuf[:, bass.ts(c, 128)],
                    rhs=xln[:, c * TH + 512:c * TH + 1024], start=(c == 0), stop=False), inc="pe")
            if j == 0:
                wait("tensor", "dv", hwv)
            do("tensor", lambda E_, j=j: E_.matmul(
                out=psAh(j % 2)[:, 0:512], lhsT=BfT[:, bass.ts(j, 128)],
                rhs=g_hw[:, 0:512], start=False, stop=True), inc="pe")
            pj = do("tensor", lambda E_, j=j: E_.matmul(
                out=psAh(j % 2)[:, 512:1024], lhsT=BfT[:, bass.ts(j, 128)],
                rhs=g_hw[:, 512:1024], start=False, stop=True), inc="pe")
            mm_done[j] = pj
            issue_wload(j + 2)
            if chain:
                wch_load(j + 2)
            rb = poolC[:, (j % 2) * 1024:(j % 2 + 1) * 1024]
            wait("sync", "dv", C["dv"] if j < 2 else evac_dv.get(j - 2, 0))
            lr = dma("sync", "lr", rb, resid_in[j * 128:(j + 1) * 128, h * TH:(h + 1) * TH])
            wait("scalar", "pe", pj)
            if j >= 1:
                wait("scalar", "dv", evac_dv[j - 1])
            aj = do("scalar", lambda E_, j=j: E_.activation(
                out=tmpc[:, 0:1024], in_=psAh(j % 2)[:, 0:1024], func=AF.Identity, scale=0.5,
                bias=bb_col[:, j:j + 1]), inc="ac")
            evac_ac[j] = aj
            if chain:
                wch_cast(j + 2)
            fout = pA(2)[:, (j % 2) * 1024:(j % 2 + 1) * 1024]
            sfs = "sf" if j % 2 == 0 else "sf1"
            wait("vector", "ac", aj)
            wait("vector", "lr", lr)
            wait("vector", sfs, C[sfs])
            vj = do("vector", lambda E_, rb=rb, fout=fout: E_.tensor_add(
                fout, tmpc[:, 0:1024], rb), inc="dv")
            evac_dv[j] = vj
            wait("scalar", "dv", vj)
            dma("scalar", sfs, resid_out[j * 128:(j + 1) * 128, h * TH:(h + 1) * TH], fout)
            if resid_out_bf is not None:
                wait("gpsimd", "dv", vj)
                wait("gpsimd", "sb_", C["sb_"])
                pk = do("gpsimd", lambda E_, fout=fout: E_.tensor_copy(pD(0)[:, 0:1024], fout), inc="po")
                wait("scalar", "po", pk)
                dma("scalar", "sb_", resid_out_bf[j * 128:(j + 1) * 128, h * TH:(h + 1) * TH], pD(0)[:, 0:1024])

    def emit_ln_stats_fm(resid_bf, dst_m, dst_r):
        sld = {}
        spe = {}
        sac = {}
        ssem = lambda c: "lr" if c % 2 == 0 else "lrs"

        def sload(c):
            if c >= NC:
                return
            wait("sync", "pe", spe.get(c - 2, C["pe"] if c < 2 else 0))
            wait("sync", "ac", sac.get(c - 2, C["ac"] if c < 2 else 0))
            sld[c] = dma("sync", ssem(c), pD(c % 2), resid_bf[c * 128:(c + 1) * 128, :])

        sload(0)
        sload(1)
        for c in range(NC):
            buf = pD(c % 2)
            wait("scalar", ssem(c), sld[c])
            if c >= 1:
                wait("scalar", "pe", spe[c - 1])
            a = do("scalar", lambda E_, buf=buf: E_.activation(out=poolF[:, 0:2048], in_=buf,
                                                               func=AF.Square), inc="ac")
            sac[c] = a
            wait("tensor", ssem(c), sld[c])
            wait("tensor", "ac", a)
            for q in range(4):
                do("tensor", lambda E_, c=c, q=q, buf=buf: E_.matmul(
                    out=psA[:, q * 512:(q + 1) * 512], lhsT=ones128,
                    rhs=buf[:, q * 512:(q + 1) * 512],
                    start=(c == 0), stop=(c == NC - 1)), inc="pe")
            for q in range(2):
                do("tensor", lambda E_, c=c, q=q: E_.matmul(
                    out=psB[:, q * 512:(q + 1) * 512], lhsT=ones128,
                    rhs=poolF[:, q * 512:(q + 1) * 512],
                    start=(c == 0), stop=(c == NC - 1)), inc="pe")
            do("tensor", lambda E_, c=c: E_.matmul(
                out=psC, lhsT=ones128, rhs=poolF[:, 1024:1536],
                start=(c == 0), stop=(c == NC - 1)), inc="pe")
            spe[c] = do("tensor", lambda E_, c=c: E_.matmul(
                out=psD, lhsT=ones128, rhs=poolF[:, 1536:2048],
                start=(c == 0), stop=(c == NC - 1)), inc="pe")
            sload(c + 2)
        pdone = C["pe"]
        wait("vector", "pe", pdone)
        sq = pA(2)
        do("vector", lambda E_: E_.tensor_scalar_mul(dst_m, psA, 1.0 / D), inc="dv")
        do("vector", lambda E_, sq=sq: E_.tensor_copy(sq[:, 0:1024], psB), inc="dv")
        do("vector", lambda E_, sq=sq: E_.tensor_copy(sq[:, 1024:1536], psC), inc="dv")
        v = do("vector", lambda E_, sq=sq: E_.tensor_copy(sq[:, 1536:2048], psD), inc="dv")
        wait("scalar", "dv", v)
        a = do("scalar", lambda E_: E_.activation(out=dst_r, in_=dst_m, func=AF.Square), inc="ac")
        wait("vector", "ac", a)
        v = do("vector", lambda E_, sq=sq: E_.scalar_tensor_tensor(
            dst_r, sq, 1.0 / D, dst_r, op0=ALU.mult, op1=ALU.subtract), inc="dv")
        wait("scalar", "dv", v)
        do("scalar", lambda E_: E_.activation(out=dst_r, in_=dst_r, func=AF.Ln, bias=EPS), inc="ac")
        do("scalar", lambda E_: E_.activation(out=dst_r, in_=dst_r, func=AF.Exp, scale=-0.5), inc="ac")

    for h in range(2):
        if max_phase >= 2 + h:
            emit_ln1_half(h)
        barrier()
        if max_phase >= 4 + h:
            emit_ffn_half(h, 1, sc_r0, sc_r1, sc_r1b)
        barrier()

    sM2, sR2 = pA(0), pA(1)
    if max_phase >= 6:
        emit_ln_stats_fm(sc_r1b, sM2, sR2)
    barrier()

    peF32 = poolE.bitcast(F32)[:, 2048:4096]
    for half in (range(2) if max_phase >= 7 else ()):
        p4ld = {}
        p4dv = {}
        p4pe = {}
        p4lg = {}
        if half == 0:
            rbv = lambda c: pC(0) if c % 2 == 0 else peF32
            lsem = lambda c: "lr" if c % 2 == 0 else "lrs"

            def p4load(c):
                if c >= NC:
                    return
                wait("sync", "dv", p4dv.get(c - 2, C["dv"] if c < 2 else 0))
                p4ld[c] = dma("sync", lsem(c), rbv(c), sc_r1[c * 128:(c + 1) * 128, :])
        else:
            lsem = lambda c: "lx" if c % 2 == 0 else "lxm"

            def p4load(c):
                if c >= NC:
                    return
                wait("sync", "pe", p4pe.get(c - 2, C["pe"] if c < 2 else 0))
                p4ld[c] = dma("sync", lsem(c), pD(c % 2), sc_xmb[c * 128:(c + 1) * 128, :])

        p4load(0)
        p4load(1)
        for c in range(NC):
            if half == 0:
                rb = rbv(c)
                wait("vector", lsem(c), p4ld[c])
                wait("vector", "lg" if c % 2 == 0 else "lg1", p4lg.get(c - 2, 0))
                if c >= 2:
                    wait("vector", "pe", p4pe[c - 2])
                do("vector", lambda E_, rb=rb: E_.tensor_sub(pA(2), rb, sM2), inc="dv")
                v = do("vector", lambda E_, c=c: E_.tensor_mul(pD(c % 2), pA(2), sR2), inc="dv")
                p4dv[c] = v
                wait("gpsimd", "dv", v)
                p4lg[c] = dma("gpsimd", "lg" if c % 2 == 0 else "lg1",
                              sc_xmb[c * 128:(c + 1) * 128, :], pD(c % 2))
                wait("tensor", "dv", v)
            else:
                wait("tensor", lsem(c), p4ld[c])
            src = pD(c % 2)[:, half * TH:(half + 1) * TH]
            pp = None
            for q in range(2):
                do("tensor", lambda E_, c=c, q=q, src=src: E_.matmul(
                    out=psA[:, q * 512:(q + 1) * 512], lhsT=xpT[:, c * 160:c * 160 + 128],
                    rhs=src[:, q * 512:(q + 1) * 512], start=(c == 0), stop=(c == NC - 1)), inc="pe")
                do("tensor", lambda E_, c=c, q=q, src=src: E_.matmul(
                    out=psB[0:16, q * 512:(q + 1) * 512], lhsT=xpT[:, c * 160 + 128:c * 160 + 144],
                    rhs=src[:, q * 512:(q + 1) * 512], start=(c == 0), stop=(c == NC - 1)), inc="pe")
                pp = do("tensor", lambda E_, c=c, q=q, src=src: E_.matmul(
                    out=psA[0:16, 1024 + q * 512:1024 + (q + 1) * 512],
                    lhsT=xpT[:, c * 160 + 144:c * 160 + 160],
                    rhs=src[:, q * 512:(q + 1) * 512], start=(c == 0), stop=(c == NC - 1)), inc="pe")
            p4pe[c] = pp
            p4load(c + 2)
        pxp = C["pe"]
        wait("vector", "pe", pxp)
        do("vector", lambda E_, half=half: E_.tensor_copy(dtin[:, half * TH:(half + 1) * TH],
                                                          psA[:, 0:1024]), inc="dv")
        do("vector", lambda E_, half=half: E_.tensor_copy(Bsb[:, half * TH:(half + 1) * TH],
                                                          psB[0:16, 0:1024]), inc="dv")
        do("vector", lambda E_, half=half: E_.tensor_copy(Csb[:, half * TH:(half + 1) * TH],
                                                          psA[0:16, 1024:2048]), inc="dv")
        barrier()

    for c in (range(NC) if max_phase >= 8 else ()):
        wait("sync", "dv", C["dv"])
        l = dma("sync", "lc", tmpc[:, 0:128], dt_proj_w[c * 128:(c + 1) * 128, :])
        wait("vector", "lc", l)
        wait("vector", "pe", C["pe"])
        v = do("vector", lambda E_: E_.tensor_copy(pD(0)[:, 0:128], tmpc[:, 0:128]), inc="dv")
        wait("tensor", "dv", v)
        p = do("tensor", lambda E_: E_.transpose(out=psDb[:, 0:128], in_=pD(0)[:, 0:128],
                                                 identity=identb), inc="pe")
        wait("vector", "pe", p)
        wait("vector", "lg", C["lg"])
        v = do("vector", lambda E_, c=c: E_.tensor_copy(pD(1)[:, 0:128], psDb[:, 0:128]), inc="dv")
        wait("gpsimd", "dv", v)
        dma("gpsimd", "lg", sc_dtp[c], pD(1)[:, 0:128])
    barrier()

    bst = poolE[:, 4096:6144]
    for n in (range(NS) if max_phase >= 9 else ()):
        for r in range(2):
            wait("tensor", "ac", C["ac"])
            wait("tensor", "dv", C["dv"])
            p = None
            for q in range(2):
                p = do("tensor", lambda E_, n=n, q=q, r=r: E_.matmul(
                    out=psB[:, q * 512:(q + 1) * 512], lhsT=indic[:, bass.ts(n, 128)],
                    rhs=Bsb[:, r * 1024 + q * 512:r * 1024 + (q + 1) * 512],
                    start=True, stop=True), inc="pe")
            wait("scalar", "pe", p)
            do("scalar", lambda E_, n=n, r=r: E_.copy(
                out=bst[:, r * 1024:(r + 1) * 1024], in_=psB), inc="ac")
        wait("vector", "ac", C["ac"])
        v = None
        for tau in range(16):
            v = do("vector", lambda E_, n=n, tau=tau: E_.tensor_mul(
                dgB(n, tau), identb, bst[:, tau * 128:(tau + 1) * 128]), inc="dv")
        for r in range(2):
            wait("tensor", "ac", C["ac"])
            p = None
            for q in range(2):
                p = do("tensor", lambda E_, n=n, q=q, r=r: E_.matmul(
                    out=psB[:, q * 512:(q + 1) * 512], lhsT=indic[:, bass.ts(n, 128)],
                    rhs=Csb[:, r * 1024 + q * 512:r * 1024 + (q + 1) * 512],
                    start=True, stop=True), inc="pe")
            wait("scalar", "pe", p)
            do("scalar", lambda E_, n=n, r=r: E_.copy(
                out=Cb_n(n)[:, r * 1024:(r + 1) * 1024], in_=psB), inc="ac")
    barrier()

    if scan_taps:
        sct = {nm: nc.dram_tensor(f"sct_{nm}", [2, 128, T], F32, kind="ExternalOutput").ap()
               for nm in ("dt", "dA", "h")}
    dt_f = pA(0)
    dA_b = (pA(1), pA(2))
    psAb = psA.bitcast(BF16)
    dbx2 = lambda th: psALL[:, th * 1024:(th + 1) * 1024]
    dty = psALL[:, 2048:4096]
    dtxT = lambda p: poolE[:, (4096 + p * 2048):(6144 + p * 2048)]
    dvhh = {}
    pyn = {}
    hc_po = {}
    stt_dv = {}
    se_cnt = {}
    pr2_po = {}
    lrs_cnt = {}
    yinit_ac = {}
    yinit_pe = {}
    pdt_cnt = {}
    pend = None

    def flush_y(pn, pnn, psid):
        wait("tensor", pn[0], pn[1])
        pyy = None
        for q in range(4):
            pyy = do("tensor", lambda E_, q=q, n=pnn: E_.matmul(
                out=dty[:, q * 512:(q + 1) * 512], lhsT=identb,
                rhs=cbuf(0)[:, q * 512:(q + 1) * 512],
                start=False, stop=(n == NS - 1)), inc="pe")
        pyn[psid] = pyy

    def emit_tail(cm):
        wait("vector", "lrs", lrs_cnt[cm])
        wait("vector", "pe", pyn[cm * NS + NS - 1])
        vr2 = do("vector", lambda E_: E_.scalar_tensor_tensor(
            pC(0), dty, 1.0, pC(0), op0=ALU.mult, op1=ALU.add), inc="dv")
        stt_dv[cm] = vr2
        wait("scalar", "dv", vr2)
        se_cnt[cm] = dma("scalar", "se", sc_r2[cm * 128:(cm + 1) * 128, :], pC(0))
        wait("gpsimd", "dv", vr2)
        wait("gpsimd", "sb2", C["sb2"])
        pr2 = do("gpsimd", lambda E_: E_.tensor_copy(r2stg, pC(0)), inc="po")
        pr2_po[cm] = pr2
        wait("scalar", "po", pr2)
        dma("scalar", "sb2", sc_r2b[cm * 128:(cm + 1) * 128, :], r2stg)
        if cm + 1 < NC:
            wait("sync", "se", se_cnt[cm])
            wait("sync", "po", pr2_po[cm])
            lrs_cnt[cm + 1] = dma("sync", "lrs", pC(0),
                                  sc_r1[(cm + 1) * 128:(cm + 2) * 128, :])

    if max_phase >= 10:
        lrs_cnt[0] = dma("sync", "lrs", pC(0), sc_r1[0:128, :])
    for c in (range(NC) if max_phase >= 10 else ()):
        dsem = "ldt0" if c % 2 == 0 else "ldt1"
        wait("sync", "pe", yinit_pe.get(c - 1, C["pe"]))
        lxm = dma("sync", "lxm", pD(1), sc_xmb[c * 128:(c + 1) * 128, :])
        if c >= 2:
            wait("sync", "pe", pdt_cnt[c - 2])
        ldt = dma("sync", dsem, dtpbuf[:, (c % 2) * 128:(c % 2 + 1) * 128], sc_dtp[c])
        acdt_h = {}
        dvdtx_h = {}
        vev_h = {}
        for hh in range(2):
            if (c * NS - 1, hh) in dvhh:
                wait("tensor", "dv", dvhh[(c * NS - 1, hh)])
            else:
                wait("tensor", "ac", C["ac"])
                wait("tensor", "dv", C["dv"])
            if hh == 0:
                wait("tensor", dsem, ldt)
            pdt = None
            for q in range(2):
                qq = hh * 2 + q
                pdt = do("tensor", lambda E_, c=c, qq=qq: E_.matmul(
                    out=psALL[:, qq * 512:(qq + 1) * 512],
                    lhsT=dtpbuf[:, (c % 2) * 128:(c % 2 + 1) * 128],
                    rhs=dtin[:, qq * 512:(qq + 1) * 512], start=True, stop=True), inc="pe")
            if hh == 1:
                pdt_cnt[c] = pdt
            wait("scalar", "pe", pdt)
            do("scalar", lambda E_, c=c, hh=hh: E_.activation(
                out=dt_f[:, hh * 1024:(hh + 1) * 1024],
                in_=psALL[:, hh * 1024:(hh + 1) * 1024],
                func=AF.Exp, bias=dtb_col[:, c:c + 1]), inc="ac")
            acdt_h[hh] = do("scalar", lambda E_, hh=hh: E_.activation(
                out=dt_f[:, hh * 1024:(hh + 1) * 1024],
                in_=dt_f[:, hh * 1024:(hh + 1) * 1024],
                func=AF.Ln, bias=1.0), inc="ac")
            wait("vector", "ac", acdt_h[hh])
            if hh == 0:
                wait("vector", "lxm", lxm)
            dvdtx_h[hh] = do("vector", lambda E_, hh=hh: E_.tensor_mul(
                pD(0)[:, hh * 1024:(hh + 1) * 1024],
                dt_f[:, hh * 1024:(hh + 1) * 1024],
                pD(1)[:, hh * 1024:(hh + 1) * 1024]), inc="dv")
            wait("tensor", "dv", dvdtx_h[hh])
            ptr = None
            for tau in range(hh * 8, hh * 8 + 8):
                ptr = do("tensor", lambda E_, tau=tau: E_.transpose(
                    out=psAb[:, tau * 128:(tau + 1) * 128], in_=pD(0)[:, bass.ts(tau, 128)],
                    identity=identb), inc="pe")
            wait("vector", "pe", ptr)
            vv = None
            for tau in range(hh * 8, hh * 8 + 8):
                vv = do("vector", lambda E_, c=c, tau=tau: E_.tensor_copy(
                    dtxT(c % 2)[:, bass.ts(tau, 128)], psAb[:, tau * 128:(tau + 1) * 128]), inc="dv")
            vev_h[hh] = vv
        wait("scalar", "dv", dvdtx_h[1])
        acdxm = do("scalar", lambda E_, c=c: E_.activation(out=pD(1), in_=pD(1), func=AF.Identity,
                                                           scale=D_col[:, c:c + 1]), inc="ac")
        yinit_ac[c] = acdxm
        if scan_taps and c < 2:
            dma("scalar", "stp", sct["dt"][c], dt_f)
        for n in range(NS):
            par = n % 2
            sid = c * NS + n
            if (sid - 2, 1) in dvhh:
                wait("scalar", "dv", dvhh[(sid - 2, 1)])
            acda = do("scalar", lambda E_, c=c, n=n, par=par: E_.activation(
                out=dA_b[par], in_=dt_f, func=AF.Exp,
                scale=A_sc[:, c * NS + n:c * NS + n + 1]), inc="ac")
            for th in range(2):
                if (sid - 1, th) in dvhh:
                    wait("tensor", "dv", dvhh[(sid - 1, th)])
                if n == 0 and th == 0:
                    wait("tensor", "dv", vev_h[1])
                pb = None
                for tt8 in range(8):
                    tau = th * 8 + tt8
                    pb = do("tensor", lambda E_, c=c, n=n, th=th, tau=tau: E_.matmul(
                        out=dbx2(th)[:, (tau % 8) * 128:((tau % 8) + 1) * 128],
                        lhsT=dtxT(c % 2)[:, bass.ts(tau, 128)],
                        rhs=dgB(n, tau), start=True, stop=True), inc="pe")
                if th == 1:
                    if n == 1:
                        wait("tensor", "ac", yinit_ac[c])
                        if (c - 1) in stt_dv:
                            wait("tensor", "dv", stt_dv[c - 1])
                        py = None
                        for q in range(4):
                            py = do("tensor", lambda E_, q=q: E_.matmul(
                                out=dty[:, q * 512:(q + 1) * 512], lhsT=identb,
                                rhs=pD(1)[:, q * 512:(q + 1) * 512], start=True, stop=False), inc="pe")
                        yinit_pe[c] = py
                    if pend is not None:
                        flush_y(*pend)
                        pend = None
                if th == 0:
                    wait("vector", "ac", acda)
                    if (sid - 2) in hc_po and hc_po[sid - 2][0] == "po":
                        wait("vector", "po", hc_po[sid - 2][1])
                wait("vector", "pe", pb)
                ini = 0.0 if th == 0 else hbuf(par)[:, 1023:1024]
                dvhh[(sid, th)] = do("vector", lambda E_, par=par, th=th, ini=ini: E_.tensor_tensor_scan(
                    hbuf(par)[:, th * 1024:(th + 1) * 1024],
                    dA_b[par][:, th * 1024:(th + 1) * 1024],
                    dbx2(th), ini, op0=ALU.mult, op1=ALU.add), inc="dv")
            if n == 0 and c > 0:
                emit_tail(c - 1)
            hce = "gpsimd" if n % 2 == 0 else "vector"
            if sid - 1 in pyn:
                wait(hce, "pe", pyn[sid - 1])
            if hce == "gpsimd":
                wait("gpsimd", "dv", dvhh[(sid, 1)])
                hc_po[sid] = ("po", do("gpsimd", lambda E_, n=n, par=par: E_.tensor_mul(
                    cbuf(par), hbuf(par), Cb_n(n)), inc="po"))
            else:
                hc_po[sid] = ("dv", do("vector", lambda E_, n=n, par=par: E_.tensor_mul(
                    cbuf(par), hbuf(par), Cb_n(n)), inc="dv"))
            pend = (hc_po[sid], n, sid)
            if scan_taps and c < 2 and n == 0:
                wait("scalar", "ac", acda)
                dma("scalar", "stp", sct["dA"][c], dA_b[par])
                vx = do("vector", lambda E_, par=par: E_.tensor_copy(tmpc[:, 0:1024], hbuf(par)[:, 0:1024]), inc="dv")
                wait("scalar", "dv", vx)
                dma("scalar", "stp", sct["h"][c][:, 0:1024], tmpc[:, 0:1024])
                wait("vector", "stp", C["stp"])
    if max_phase >= 10:
        if pend is not None:
            flush_y(*pend)
            pend = None
        emit_tail(NC - 1)
    barrier()

    sM3, sR3 = pA(0), pA(1)
    if max_phase >= 11:
        emit_ln_stats_fm(sc_r2b, sM3, sR3)
    barrier()

    if max_phase >= 12:
        l1 = dma("gpsimd", "lg", BfT, sc_bft)
        l2 = dma("scalar", "se", AfT, sc_aft)
        l3 = dma("scalar", "se", WgTb, sc_wgt)
    barrier()
    for h in range(2):
        if max_phase >= 13:
            emit_ffn_half(h, 2, sc_r2, sc_r3, None, sM=sM3, sR=sR3)
        barrier()

    p8pe = {}
    p8st = {}
    p8ld = {}
    stgv = lambda tt: poolA[:, 0:4096] if tt % 2 == 0 else stg_big
    xtv = lambda tt: (big[:, 0:8192] if tt % 2 == 0 else big[:, 16384:24576]).bitcast(F32)
    lsem8 = lambda tt: "lx" if tt % 2 == 0 else "lg1"
    leng8 = lambda tt: "sync" if tt % 2 == 0 else "gpsimd"
    ssem8 = lambda tt: "so" if tt % 2 == 0 else "stp"

    def p8load(tt):
        if tt >= NT:
            return
        wait(leng8(tt), "pe", p8pe.get(tt - 2, C["pe"] if tt < 2 else 0))
        p8ld[tt] = dma(leng8(tt), lsem8(tt), stgv(tt).rearrange("p (c t) -> p c t", t=128),
                       sc_r3[:, tt * 128:(tt + 1) * 128].rearrange("(c p) t -> p c t", p=128))

    if max_phase >= 14:
        p8load(0)
        p8load(1)
    for tt in (range(NT) if max_phase >= 14 else ()):
        stg = stgv(tt)
        xt = xtv(tt)
        wait("tensor", lsem8(tt), p8ld[tt])
        for b4 in range(8):
            p = None
            for i in range(4):
                c = b4 * 4 + i
                dst = psC if i < 2 else psD
                sl = (i % 2) * 256
                p = do("tensor", lambda E_, c=c, dst=dst, sl=sl, stg=stg: E_.transpose(
                    out=dst[:, sl:sl + 128], in_=stg[:, bass.ts(c, 128)],
                    identity=identf), inc="pe")
            if b4 == 7:
                p8pe[tt] = p
            wait("vector", "pe", p)
            if b4 == 0:
                wait("vector", ssem8(tt), p8st.get(tt - 2, 0))
            vv = None
            for i in range(4):
                c = b4 * 4 + i
                src = psC if i < 2 else psD
                sl = (i % 2) * 256
                vv = do("vector", lambda E_, c=c, src=src, sl=sl, xt=xt: E_.tensor_copy(
                    xt[:, bass.ts(c, 128)], src[:, sl:sl + 128]), inc="dv")
            wait("tensor", "dv", vv)
        wait("scalar", "dv", vv)
        p8st[tt] = dma("scalar", ssem8(tt), out_d[tt * 128:(tt + 1) * 128, :], xt)
        p8load(tt + 2)

    tapmap = {"r0": (sc_r0, False), "r1": (sc_r1, False), "r2": (sc_r2, False),
              "xmb": (sc_xmb, True)}
    for name in debug_taps:
        srcap, isbf = tapmap[name]
        tap = nc.dram_tensor(f"tap_{name}", [D, T], F32, kind="ExternalOutput").ap()
        for c in range(NC):
            wait("sync", "stp", C["stp"])
            if isbf:
                lc = dma("sync", "lx", pD(0), srcap[c * 128:(c + 1) * 128, :])
                wait("vector", "lx", lc)
                v = do("vector", lambda E_: E_.tensor_copy(pA(0), pD(0)), inc="dv")
            else:
                lc = dma("sync", "lx", pD(0).bitcast(F32)[:, 0:1024],
                         srcap[c * 128:(c + 1) * 128, 0:1024])
                lc2 = dma("sync", "lxm", pD(1).bitcast(F32)[:, 0:1024],
                          srcap[c * 128:(c + 1) * 128, 1024:2048])
                wait("vector", "lx", lc)
                wait("vector", "lxm", lc2)
                do("vector", lambda E_: E_.tensor_copy(pA(0)[:, 0:1024],
                                                       pD(0).bitcast(F32)[:, 0:1024]), inc="dv")
                v = do("vector", lambda E_: E_.tensor_copy(pA(0)[:, 1024:2048],
                                                           pD(1).bitcast(F32)[:, 0:1024]), inc="dv")
            wait("scalar", "dv", v)
            dma("scalar", "stp", tap[c * 128:(c + 1) * 128, :], pA(0))

    wait("gpsimd", "so", C["so"])
    wait("gpsimd", "stp", C["stp"])

    with nc.Block() as block:
        @block.sync
        def _(sync):
            for f in prog["sync"]:
                f(sync)

        @block.tensor
        def _(tensor):
            for f in prog["tensor"]:
                f(tensor)

        @block.vector
        def _(vector):
            for f in prog["vector"]:
                f(vector)

        @block.scalar
        def _(scalar):
            for f in prog["scalar"]:
                f(scalar)

        @block.gpsimd
        def _(gpsimd):
            for f in prog["gpsimd"]:
                f(gpsimd)

    return nc


def _consts():
    ident = np.eye(128, dtype=np.float32)
    ident_b = ident.copy()
    ones_b = np.ones((128, 128), dtype=np.float32)
    indic = np.zeros((16, 16 * 128), dtype=np.float32)
    for n in range(16):
        indic[n, n * 128:(n + 1) * 128] = 1.0
    expand = np.zeros((16, 128), dtype=np.float32)
    for e in range(E):
        expand[e, e * RK:(e + 1) * RK] = 1.0
    return ident, ident_b, ones_b, indic, expand


def kernel(L0_attn_x, L0_attn_A_log, L0_attn_D, L0_attn_x_proj_w, L0_attn_dt_proj_w,
           L0_attn_dt_proj_b, L0_ffn_W_base, L0_ffn_b_base, L0_ffn_W_gate, L0_ffn_W_thr,
           L0_ffn_b_thr, L0_ffn_A_w, L0_ffn_B_w, _debug_taps=(), _max_phase=99,
           _n_cores=8, _nc_cache={}):
    ident, ident_b, ones_b, indic, expand = _consts()
    x = np.ascontiguousarray(np.asarray(L0_attn_x, dtype=np.float32))
    cf = lambda a: np.ascontiguousarray(np.asarray(a, dtype=np.float32))
    shared = {
        "A_log": cf(L0_attn_A_log), "Dskip": cf(L0_attn_D),
        "x_proj_w": cf(L0_attn_x_proj_w), "dt_proj_w": cf(L0_attn_dt_proj_w),
        "dt_proj_b": cf(L0_attn_dt_proj_b), "W_base": cf(L0_ffn_W_base),
        "b_base": cf(L0_ffn_b_base), "W_gate": cf(L0_ffn_W_gate),
        "W_thr": cf(L0_ffn_W_thr), "b_thr": cf(L0_ffn_b_thr),
        "A_w": cf(L0_ffn_A_w).reshape(E * RK, D),
        "B_w": cf(L0_ffn_B_w).reshape(E * D, RK),
        "ident_f": ident, "ident_b": ident_b, "ones_b": ones_b,
        "indic_b": indic, "expand_b": expand,
    }
    key = (tuple(_debug_taps), _max_phase)
    if key not in _nc_cache:
        _nc_cache[key] = build_program(debug_taps=_debug_taps, max_phase=_max_phase)
    nc = _nc_cache[key]
    in_maps = [dict(shared, x=np.ascontiguousarray(x[b])) for b in range(_n_cores)]
    res = run_bass_kernel_spmd(nc, in_maps, core_ids=list(range(_n_cores)))
    out = np.stack([np.asarray(r["out"], dtype=np.float32) for r in res.results], axis=0)
    if _n_cores < 8:
        out = np.concatenate([out] + [np.zeros_like(out[:1])] * (8 - _n_cores), axis=0)
    if _debug_taps:
        taps = [{k: np.asarray(r[f"tap_{k}"]) for k in _debug_taps} for r in res.results]
        return out, taps
    return out
